# version 37
# baseline (speedup 1.0000x reference)
import numpy as np
from contextlib import ExitStack

import concourse.bass as bass
import concourse.mybir as mybir
from concourse.bass_utils import run_bass_kernel_spmd

F32 = mybir.dt.float32
BF16 = mybir.dt.bfloat16
ALU = mybir.AluOpType
AF = mybir.ActivationFunctionType
AX = mybir.AxisListType

RND = 12582912.0
QMAX = 127.0
EPS = 1e-5

ENGS = ("sync", "scalar", "vector", "tensor", "gpsimd")
SEMS = ("TE", "VE", "SE", "BAR", "GE", "CC0", "CC1", "CC2", "CC3",
        "VEC", "XLD", "XL1", "XL2", "XL3", "WA0", "WA1", "WA2", "WA3", "WF0", "WF1", "TKL", "VWL", "MXL0", "MXL1",
        "XR0", "XR1", "X20", "X21", "WG0", "WG1", "WDL", "WF2", "WF3",
        "VWL0", "VWL1", "VWL2", "VWL3", "VWL4", "VWL5", "VWL6", "VWL7",
        "WV", "WO",
        "VS0", "VS1", "TKS", "XM0", "XM1", "XM2", "MXS0", "MXS1", "WS0", "WS1",
        "YS0", "YS1", "CSS", "CSL", "CC4")


def full_cfg():
    return dict(T=2048, D=1024, I=2816, GROUP=4, CORES=8)


def test_cfg():
    return dict(T=256, D=256, I=384, GROUP=4, CORES=8)


def test2_cfg():
    return dict(T=512, D=512, I=640, GROUP=4, CORES=8, TOKC=256, QG=2)


_ESEM = {"vector": "VE", "scalar": "SE", "tensor": "TE"}


class Em:
    def __init__(self):
        self.streams = {e: [] for e in ENGS}
        self.cnt = {s: 0 for s in SEMS}
        self.last = {"vector": 0, "scalar": 0}
        self.sems = None

    def c(self, s):
        return self.cnt[s]

    def wait(self, e, s, v):
        if v > 0:
            self.streams[e].append(("w", s, v))

    def ins(self, e, fn, wait=(), inc=None, n=1):
        wait = [(s, v) for (s, v) in wait if v > 0]
        if e in ("vector", "scalar") and (inc is None or inc == _ESEM[e]):
            sem = _ESEM[e]
            if self.last[e] > 0:
                wait.append((sem, self.last[e]))
            inc, n = sem, 1
        wm = {}
        for s, v in wait:
            wm[s] = max(wm.get(s, 0), v)
        wait = list(wm.items())
        if inc is not None:
            self.cnt[inc] += n
        self.streams[e].append(("i", fn, inc, n, wait))
        if e in ("vector", "scalar") and inc == _ESEM[e]:
            self.last[e] = self.cnt[inc]
        return self.cnt[inc] if inc is not None else None

    def raw(self, e, fn, wait=()):
        wait = [(s, v) for (s, v) in wait if v > 0]
        self.streams[e].append(("i", fn, None, 0, wait))

    def barrier(self):
        for e in ENGS:
            if e in ("tensor", "vector", "scalar", "gpsimd"):
                self.raw(e, lambda eng: eng.drain())
            self.cnt["BAR"] += 1
            self.raw(e, lambda eng: eng.sem_inc(self.sems["BAR"], 1))
        tgt = self.cnt["BAR"]
        for e in ENGS:
            self.wait(e, "BAR", tgt)

    def replay(self, block):
        for e in ENGS:
            def body(eng, _e=e):
                for item in self.streams[_e]:
                    if item[0] == "w":
                        eng.wait_ge(self.sems[item[1]], item[2])
                    else:
                        _, fn, inc, n, wait = item
                        for s, v in wait[1:]:
                            eng.wait_ge(self.sems[s], v)
                        ins = fn(eng)
                        if wait:
                            s, v = wait[0]
                            ins._wait_ge(self.sems[s], v)
                        if inc is not None:
                            ins.then_inc(self.sems[inc], n)
            getattr(block, e)(body)


class Layout:
    PHASES = ("perm", "A1", "A2", "A3", "B", "C1", "C2", "D")

    def __init__(self, nc, limit=None):
        self.nc = nc
        base = nc._sbuf_addr_for_side("left")
        self.base = (base + 63) & ~63
        avail = nc._sbuf_addr_for_side("right") - self.base - 6144
        self.limit = min(limit, avail) if limit else avail
        nc.alloc_sbuf_tensor("manual_arena", [128, (self.limit + self.base - base)],
                             mybir.dt.uint8)
        self.placed = []

    def at(self, name, shape, dtype, phases=("perm", "D")):
        nbytes = int(np.prod(shape[1:])) * mybir.dt.size(dtype)
        nbytes = (nbytes + 63) & ~63
        p0 = self.PHASES.index(phases[0])
        p1 = self.PHASES.index(phases[1])
        off = 0
        while True:
            clash = None
            for (o, e, q0, q1, _) in self.placed:
                if o < off + nbytes and off < e and not (p1 < q0 or q1 < p0):
                    clash = max(clash or 0, e)
            if clash is None:
                break
            off = (clash + 63) & ~63
        assert off + nbytes <= self.limit, (
            f"SBUF overflow placing {name}: {off}+{nbytes} > {self.limit}")
        self.placed.append((off, off + nbytes, p0, p1, name))
        return self.nc.alloc_sbuf_tensor_at(name, list(shape), dtype,
                                            offset=self.base + off)

    def range_of(self, name):
        for (o, e, _, _, n) in self.placed:
            if n == name:
                return (o, e)
        raise KeyError(name)


def _wv(tok):
    return tok if isinstance(tok, tuple) else ("VE", tok)


def build_program(cfg):
    T, D, I, GROUP, CORES = cfg["T"], cfg["D"], cfg["I"], cfg["GROUP"], cfg["CORES"]
    N = T * GROUP
    nT = T // 128
    nK = N // 128
    nC = D // 128
    OC = min(512, D)
    nOC = D // OC
    TOKC = cfg.get("TOKC", min(512, T))
    nTC = T // TOKC
    nI = I // 128
    QG = cfg.get("QG", min(8 // nOC, nT))
    nQG = nT // QG
    ICH = []
    rem = I
    while rem > 0:
        c = min(1024, rem)
        ICH.append(c)
        rem -= c
    assert nK <= 128 and nTC <= 4 and nC <= 8

    nc = bass.Bass("TRN2", target_bir_lowering=False, debug=False)

    def din(name, shape):
        return nc.dram_tensor(name, list(shape), F32, kind="ExternalInput")

    x_h = din("L0_attn_x", [T, D])
    w_h = {"q": din("L0_attn_Wq", [D, D]), "k": din("L0_attn_Wk", [D, D]),
           "v": din("L0_attn_Wv", [D, D]), "o": din("L0_attn_Wo", [D, D])}
    vec_h = []
    for p in ("q", "k", "v"):
        for f in ("g", "b", "m", "v"):
            vec_h.append(din(f"L0_attn_{f}{p}", [D]))
    bo_h = din("L0_attn_bo", [D])
    clipx_h = din("L0_ffn_clip_x", [1])
    clipd_h = din("L0_ffn_clip_down", [1])
    wg_h = din("L0_ffn_w_gate", [I, D])
    wu_h = din("L0_ffn_w_up", [I, D])
    wd_h = din("L0_ffn_w_down", [D, I])
    out_h = nc.dram_tensor("out", [T, D], F32, kind="ExternalOutput")

    vw_in = nc.dram_tensor("vw_in", [T, D], BF16)
    vw_out = nc.dram_tensor("vw_out", [N, D], BF16)
    tk_in = nc.dram_tensor("tk_in", [T], F32)
    tk_out = nc.dram_tensor("tk_out", [N], F32)
    mx_in = [nc.dram_tensor(f"mx{i}_in", [8], F32) for i in range(2)]
    mx_out = [nc.dram_tensor(f"mx{i}_out", [8], F32) for i in range(2)]
    cs_in = nc.dram_tensor("cs_in", [D], F32)
    cs_out = nc.dram_tensor("cs_out", [D], F32)
    xmid_h = nc.dram_tensor("xmid_h", [T, D], F32)
    wT_sp = {"g": nc.dram_tensor("wgT_sp", [nI, nC, 128, 128], BF16),
             "u": nc.dram_tensor("wuT_sp", [nI, nC, 128, 128], BF16),
             "d": nc.dram_tensor("wdT_sp", [nI, nC, 128, 128], BF16)}

    rg_batch = [list(range(g * GROUP, (g + 1) * GROUP)) for g in range(CORES // GROUP)]
    rg_all = [list(range(CORES))]

    L = Layout(nc)
    PM = ("perm", "D")
    id_f = L.at("id_f", [128, 128], F32, PM)
    id_b = L.at("id_b", [128, 128], BF16, PM)
    ones_cb = L.at("ones_cb", [128, 1], BF16, PM)
    ones_rf = L.at("ones_rf", [1, 128], F32, PM)
    rowpool = L.at("rowpool", [128, max(T, 1024)], F32, PM)
    p0ex = L.at("p0ex", [1, 64], F32, PM)
    crow = L.at("crow", [1, D], F32, ("B", "B"))
    borow = L.at("borow", [1, D], F32, ("B", "B"))
    incols = L.at("incols", [128, 12 * nC], F32, PM)
    afcols = L.at("afcols", [128, 6 * nC], F32, PM)
    NSC = 3 * max(nC, nT) + 8
    tmpc = L.at("tmpc", [128, NSC], F32, PM)
    tK_sb = L.at("tK_sb", [128, nK], F32, PM)
    tK2p = L.at("tK2p", [128, nK], F32, PM)
    tK2n = L.at("tK2n", [128, nK], F32, PM)
    stat = L.at("stat", [128, 8 * nT], F32, PM)
    rmg = L.at("rmg", [128, nI], F32, PM)
    rmu = L.at("rmu", [128, nI], F32, PM)
    rmd = L.at("rmd", [128, nC], F32, PM)
    cgc = L.at("cgc", [128, nI], F32, PM)
    cuc = L.at("cuc", [128, nI], F32, PM)
    ccol = L.at("ccol", [128, 8], F32, PM)
    wldf = [L.at(f"wldf{i}", [128, 1024], F32, ("perm", "A3")) for i in range(4)]
    wintf = [L.at(f"wintf{i}", [128, 1024], BF16, ("perm", "A3")) for i in range(2)]
    wts = [L.at(f"wts{i}", [128, 1024], BF16, ("perm", "A3")) for i in range(2)]
    sqd = L.at("sqd", [128, 1024], F32, PM)

    AA = ("A1", "A3")
    xln1T = L.at("xln1T", [128, nC * T], BF16, AA)
    wb = [L.at(f"wb{i}", [128, nC * D], BF16, AA) for i in range(2)]
    wlda = [L.at(f"wlda{i}", [128, 1024], F32, ("perm", "A3")) for i in range(3)]
    qbuf = L.at("qbuf", [128, nC * T], BF16, ("A2", "A3"))
    vwring = [L.at(f"vwring{i}", [128, D], BF16, ("A2", "A2")) for i in range(2)]
    xbuf = L.at("xbuf", [128, nT * D], F32, ("A1", "A1"))
    vtb = L.at("vtb", [128, nC * T], BF16, ("A2", "A2"))

    BB = ("B", "B")
    tQb = L.at("tQb", [128, T], F32, BB)
    ub = [L.at(f"ub{i}", [128, QG * 128], BF16, BB) for i in range(3)]
    wub = [L.at(f"wub{i}", [128, QG * 128], BF16, BB) for i in range(3)]
    mb = [L.at(f"mb{i}", [128, QG * 128], mybir.dt.uint8, BB) for i in range(3)]
    db = [L.at(f"db{i}", [128, QG * 128], BF16, BB) for i in range(3)]
    xrel = [L.at(f"xrel{i}", [128, D], F32, BB) for i in range(2)]
    xmr = [L.at(f"xmr{i}", [128, D], F32, BB) for i in range(4)]
    vw_sb = L.at("vw_sb", [128, nK * D], BF16, BB)

    xrel2 = [L.at(f"xrel2{i}", [128, D], F32, ("C1", "D")) for i in range(2)]
    ybuf = [L.at(f"ybuf{i}", [128, D], F32, ("C1", "D")) for i in range(2)]
    h_sb = L.at("h_sb", [128, nI * T], BF16, ("C1", "D"))
    wff = [L.at(f"wff{i}", [128, 2 * nC * 128], BF16, ("C1", "C1")) for i in range(2)]
    sg = L.at("sg", [128, T], F32, ("C1", "C1"))
    rxT = L.at("rxT", [128, nC * T], BF16, ("C1", "C1"))
    rxb = [L.at(f"rxb{i}", [128, D], BF16, ("C1", "C1")) for i in range(2)]
    wdT_sb = L.at("wdT_sb", [128, nI * D], BF16, ("C2", "D"))
    vd2 = L.at("vd2", [128, min(1024, T)], F32, ("C2", "D"))
    tkld = L.at("tkld", [128, 128], F32, ("A3", "B"))

    ps = [nc.alloc_psum_tensor(f"ps{i}", [128, 512], F32) for i in range(8)]

    em = Em()

    def dma(q, s, out_ap, in_ap, wait=()):
        v = em.ins(q, lambda eng, o=out_ap, i=in_ap: eng.dma_start(out=o, in_=i),
                   wait=wait, inc=s, n=16)
        return (s, v)

    def dma_g(s, out_ap, in_ap, wait=()):
        return dma("gpsimd", s, out_ap, in_ap, wait=wait)

    def rsqrt_cols(dst, src, n, sc0):
        t0 = tmpc[:, sc0:sc0 + n]
        t1 = tmpc[:, sc0 + n:sc0 + 2 * n]
        t2 = tmpc[:, sc0 + 2 * n:sc0 + 3 * n]
        em.ins("scalar", lambda e: e.activation(t0, src, AF.Sqrt),
               wait=[("VE", em.c("VE"))], inc="SE")
        se = em.c("SE")
        em.ins("vector", lambda e: e.reciprocal(t1, t0), wait=[("SE", se)])
        em.ins("vector", lambda e: e.tensor_mul(t2, t1, t1))
        em.ins("vector", lambda e: e.tensor_mul(t2, t2, src))
        em.ins("vector", lambda e: e.tensor_scalar(t2, t2, -0.5, 1.5, ALU.mult, ALU.add))
        em.ins("vector", lambda e: e.tensor_mul(dst, t1, t2), inc="VE")

    em.ins("gpsimd", lambda e: e.iota(
        wldf[1][:, 512:640], [[-1, 128]], base=0, channel_multiplier=1,
        allow_small_or_imprecise_dtypes=True))
    em.ins("gpsimd", lambda e: e.drain())
    em.cnt["GE"] += 1
    em.ins("gpsimd", lambda e: e.sem_inc(em.sems["GE"], 1))
    cc_iota = em.c("GE")
    em.ins("vector", lambda e: e.tensor_single_scalar(
        id_f[:, :], wldf[1][:, 512:640], 0.0, ALU.is_equal),
        wait=[("GE", cc_iota)], inc="VE")
    em.ins("vector", lambda e: e.tensor_copy(id_b[:, :], id_f[:, :]))
    ve_id = em.c("VE")
    em.ins("vector", lambda e: e.memset(ones_cb[:, :], 1.0))
    em.ins("vector", lambda e: e.memset(ones_rf[:, :], 1.0))

    for k in range(12):
        s, kk = (0, k) if k < 8 else (1, k - 8)
        dma("scalar", "VEC", wldf[s][0:nC, kk * 128:(kk + 1) * 128],
            vec_h[k].ap().rearrange("(a b) -> a b", b=128))
    dma("scalar", "VEC", p0ex[:, 0:1], clipx_h.ap().rearrange("(a b) -> a b", a=1))
    ld_sc = dma("scalar", "VEC", p0ex[:, 1:2],
                clipd_h.ap().rearrange("(a b) -> a b", a=1))

    xs = x_h.ap().rearrange("(t p) d -> t p d", p=128)
    step = max(1, nT // 4)
    ld_x = [0] * nT
    for ci, t0 in enumerate(range(0, nT, step)):
        t1 = min(nT, t0 + step)
        sname = "XLD" if ci == 0 else f"XL{ci}"
        tok = dma("sync", sname,
                  xbuf[:, t0 * D:t1 * D].rearrange("p (t d) -> p t d", d=D),
                  x_h.ap()[t0 * 128:t1 * 128, :].rearrange(
                      "(t p) d -> p t d", p=128))
        for t in range(t0, t1):
            ld_x[t] = tok

    for k in range(12):
        s, kk = (0, k) if k < 8 else (1, k - 8)
        em.ins("tensor", lambda e, s=s, kk=kk, k=k: e.transpose(
            ps[5][:, k * nC:(k + 1) * nC],
            wldf[s][0:nC, kk * 128:(kk + 1) * 128], id_f[0:nC, 0:nC]),
            wait=[ld_sc] + ([("VE", ve_id)] if k == 0 else []),
            inc=("TE" if k == 11 else None))
    te_vec = em.c("TE")
    ve_incols = 0

    em.barrier()

    spill_st = []
    wldf_ve = [[("TE", te_vec)], [("TE", te_vec), ("VE", ve_id)],
               [("TE", te_vec)], [("TE", te_vec)]]
    wpar = [0]
    pkid = [0]

    def ffnw_pack(wi_ap, nblk, dst_ap, ve_q):
        myid = pkid[0]
        pkid[0] += 1
        bank = ps[7].bitcast(BF16)
        wt = wts[myid % 2]
        for r0 in range(0, nblk, 4):
            jw = min(4, nblk - r0)
            for j in range(jw):
                em.ins("tensor", lambda e, j=j, r0=r0, bank=bank, wi_ap=wi_ap:
                       e.transpose(
                           bank[:, j * 128:(j + 1) * 128],
                           wi_ap[:, (r0 + j) * 128:(r0 + j + 1) * 128], id_b[:, :]),
                       wait=([("VE", ve_q), ("SE", spill_evac_se[0])]
                             if j == 0 else []),
                       inc=("TE" if j == jw - 1 else None))
            te = em.c("TE")
            em.ins("scalar", lambda e, wt=wt, bank=bank, r0=r0, jw=jw: e.copy(
                wt[:, r0 * 128:(r0 + jw) * 128], bank[:, 0:jw * 128]),
                wait=[("TE", te)] +
                     ([spill_st[myid - 2]] if r0 == 0 and myid >= 2 else []),
                inc="SE")
            spill_evac_se[0] = em.c("SE")
        se = em.c("SE")
        stv = dma("gpsimd", f"WS{myid % 2}", dst_ap.rearrange("c p j -> p c j"),
                  wt[:, 0:nblk * 128].rearrange("p (c j) -> p c j", j=128),
                  wait=[("SE", se)])
        spill_st.append(stv)

    spill_evac_se = [0]

    gu_state = {}

    def gu_load(kind, it):
        wh = wg_h if kind == "g" else wu_h
        par = wpar[0] % 4
        wpar[0] += 1
        ld = dma("sync", f"WF{par}", wldf[par][:, 0:D],
                 wh.ap().rearrange("(i p) d -> i p d", p=128)[it],
                 wait=list(wldf_ve[par]))
        gu_state[(kind, it)] = {"par": par, "ld": ld}

    guq = [0]
    wintf_ve = [[], []]

    def gu_quant(kind, it):
        st = gu_state[(kind, it)]
        par = st["par"]
        wq = guq[0] % 2
        guq[0] += 1
        st["wq"] = wq
        rm = rmg if kind == "g" else rmu
        rmc = rm[:, it:it + 1]
        tr = tmpc[:, NSC - 2:NSC - 1]
        em.ins("vector", lambda e, rmc=rmc, par=par: e.reduce_max(
            rmc, wldf[par][:, 0:D], AX.X, apply_absolute_value=True),
            wait=[st["ld"]])
        em.ins("vector", lambda e, rmc=rmc, tr=tr: e.tensor_scalar_mul(tr, rmc, 1.0 / QMAX))
        em.ins("vector", lambda e, tr=tr: e.reciprocal(tr, tr))
        em.ins("vector", lambda e, par=par, tr=tr: e.tensor_scalar(
            wldf[par][:, 0:D], wldf[par][:, 0:D], tr, RND, ALU.mult, ALU.add))
        em.ins("vector", lambda e, par=par: e.tensor_scalar(
            wldf[par][:, 0:D], wldf[par][:, 0:D], RND, -QMAX - 1.0,
            ALU.subtract, ALU.max))
        em.ins("vector", lambda e, par=par, wq=wq: e.tensor_single_scalar(
            wintf[wq][:, 0:D], wldf[par][:, 0:D], QMAX, ALU.min),
            wait=list(wintf_ve[wq]), inc="VE")
        st["veq"] = em.c("VE")
        wldf_ve[par] = [("VE", st["veq"])]

    def gu_pack(kind, it):
        st = gu_state.pop((kind, it))
        sp = wT_sp[kind]
        ffnw_pack(wintf[st["wq"]][:, 0:nC * 128], nC, sp.ap()[it], st["veq"])
        wintf_ve[st["wq"]] = [("TE", em.c("TE"))]

    def ffnw_d(th):
        tr = tmpc[:, NSC - 2:NSC - 1]
        rsd = tmpc[:, NSC - 1:NSC]
        pars = []
        base = 0
        for ci, cl in enumerate(ICH):
            par = wpar[0] % 4
            wpar[0] += 1
            ld = dma("sync", f"WF{par}", wldf[par][:, 0:cl],
                     wd_h.ap()[th * 128:(th + 1) * 128, base:base + cl],
                     wait=list(wldf_ve[par]))
            dstc = rmd[:, th:th + 1] if ci == 0 else tmpc[:, NSC - 3:NSC - 2]
            em.ins("vector", lambda e, dstc=dstc, par=par, cl=cl: e.reduce_max(
                dstc, wldf[par][:, 0:cl], AX.X, apply_absolute_value=True),
                wait=[ld])
            if ci > 0:
                em.ins("vector", lambda e, th=th: e.tensor_max(
                    rmd[:, th:th + 1], rmd[:, th:th + 1], tmpc[:, NSC - 3:NSC - 2]))
            em.ins("vector", lambda e: e.engine_nop(), inc="VE")
            pars.append((par, cl, base))
            base += cl
        em.ins("vector", lambda e, th=th, tr=tr: e.tensor_scalar_mul(
            tr, rmd[:, th:th + 1], 1.0 / QMAX))
        em.ins("vector", lambda e, tr=tr: e.reciprocal(tr, tr))
        em.ins("vector", lambda e, th=th, rsd=rsd: e.tensor_scalar_mul(
            rsd, rmd[:, th:th + 1], 1.0 / QMAX), inc="VE")
        for ci2, (par, cl, base) in enumerate(pars):
            wq2 = ci2 % 2
            em.ins("vector", lambda e, par=par, cl=cl, tr=tr: e.tensor_scalar(
                wldf[par][:, 0:cl], wldf[par][:, 0:cl], tr, RND, ALU.mult, ALU.add))
            em.ins("vector", lambda e, par=par, cl=cl: e.tensor_scalar(
                wldf[par][:, 0:cl], wldf[par][:, 0:cl], RND, -QMAX - 1.0,
                ALU.subtract, ALU.max))
            em.ins("vector", lambda e, par=par, cl=cl: e.tensor_single_scalar(
                wldf[par][:, 0:cl], wldf[par][:, 0:cl], QMAX, ALU.min))
            em.ins("vector", lambda e, par=par, cl=cl, rsd=rsd, wq2=wq2:
                   e.tensor_scalar_mul(
                       wintf[wq2][:, 0:cl], wldf[par][:, 0:cl], rsd),
                   wait=list(wintf_ve[wq2]), inc="VE")
            ve_q = em.c("VE")
            nblk = cl // 128
            ic0 = base // 128
            ffnw_pack(wintf[wq2][:, 0:nblk * 128], nblk,
                      wT_sp["d"].ap()[ic0:ic0 + nblk, th], ve_q)
            wintf_ve[wq2] = [("TE", em.c("TE"))]
            wldf_ve[par] = [("VE", ve_q)]

    gu_units = [("g", it) for it in range(nI)] + [("u", it) for it in range(nI)]
    ffnw_sched = []
    for i in range(len(gu_units) + 2):
        if i < len(gu_units):
            ffnw_sched.append(("L",) + gu_units[i])
        if i >= 1 and i - 1 < len(gu_units):
            ffnw_sched.append(("Q",) + gu_units[i - 1])
        if i >= 2 and i - 2 < len(gu_units):
            ffnw_sched.append(("P",) + gu_units[i - 2])
    ffnw_sched += [("D", None, th) for th in range(nC)]
    ffnw_units = ffnw_sched
    ffnw_pos = [0]

    def ffnw_tick(n=4):
        flush = n > 8
        for _ in range(n):
            if ffnw_pos[0] >= len(ffnw_sched):
                return
            stg, kind, idx = ffnw_sched[ffnw_pos[0]]
            if stg == "D" and not flush:
                return
            ffnw_pos[0] += 1
            if stg == "L":
                gu_load(kind, idx)
            elif stg == "Q":
                gu_quant(kind, idx)
            elif stg == "P":
                gu_pack(kind, idx)
            else:
                ffnw_d(idx)

    wslot = {"v": 0, "o": 1, "k": 1, "q": 0}
    ve_w = {}
    wlda_ve = [0, 0, 0]
    slot_ve = {0: 0, 1: 0}

    def conv_attn_w(nm, extra=()):
        slot = wslot[nm]
        ws = w_h[nm].ap().rearrange("(c p) d -> c p d", p=128)
        for c in range(nC):
            sl = c % 3
            ld = dma("scalar", ("WA0", "WA2", "WA3")[sl], wlda[sl][:, 0:D], ws[c],
                     wait=[("VE", wlda_ve[sl])] + (list(extra) if c == 0 else []))
            em.ins("vector", lambda e, nm=nm, slot=slot, c=c, sl=sl: e.tensor_copy(
                wb[slot][:, c * D:(c + 1) * D], wlda[sl][:, 0:D]),
                wait=[ld, ("TE", slot_ve[slot])], inc="VE")
            wlda_ve[sl] = em.c("VE")
        ve_w[nm] = em.c("VE")

    wvo = [(nm, c) for nm in ("v", "o") for c in range(nC)]
    wvo_ld = {}
    for i in range(min(4, len(wvo))):
        nm, c = wvo[i]
        par = i % 4
        wvo_ld[i] = (par, dma("scalar", f"WF{par}", wldf[par][:, 0:D],
                              w_h[nm].ap()[c * 128:(c + 1) * 128, :],
                              wait=list(wldf_ve[par])))

    def emit_wvo_casts(i0, i1):
        for i in range(i0, i1):
            nm, c = wvo[i]
            slot = wslot[nm]
            par, ld = wvo_ld[i]
            em.ins("vector", lambda e, slot=slot, c=c, par=par: e.tensor_copy(
                wb[slot][:, c * D:(c + 1) * D], wldf[par][:, 0:D]),
                wait=[ld], inc="VE")
            wldf_ve[par] = [("VE", em.c("VE"))]
            ve_w[nm] = em.c("VE")
            j = i + 4
            if j < len(wvo):
                nm2, c2 = wvo[j]
                par2 = j % 4
                wvo_ld[j] = (par2, dma("scalar", f"WF{par2}", wldf[par2][:, 0:D],
                                       w_h[nm2].ap()[c2 * 128:(c2 + 1) * 128, :],
                                       wait=list(wldf_ve[par2])))


    sucol1 = stat[:, 0:nT]
    sscol1 = stat[:, nT:2 * nT]
    mcol1 = stat[:, 3 * nT:4 * nT]
    rcol1 = stat[:, 4 * nT:5 * nT]
    se_sq_t = [0] * nT
    ve_ln = [0] * nT
    LG = min(4, nT)

    def emit_afcols_unit(j):
        g = incols[:, (4 * j + 0) * nC:(4 * j + 1) * nC]
        b = incols[:, (4 * j + 1) * nC:(4 * j + 2) * nC]
        m = incols[:, (4 * j + 2) * nC:(4 * j + 3) * nC]
        v = incols[:, (4 * j + 3) * nC:(4 * j + 4) * nC]
        s4 = afcols[:, (2 * j) * nC:(2 * j + 1) * nC]
        b4 = afcols[:, (2 * j + 1) * nC:(2 * j + 2) * nC]
        veps = tmpc[:, 3 * nC:4 * nC]
        em.ins("vector", lambda e, v=v, veps=veps: e.tensor_scalar_add(
            veps, v, EPS), wait=[("VE", ve_incols)], inc="VE")
        rsqrt_cols(s4, veps, nC, 0)
        em.ins("vector", lambda e, s4=s4, g=g: e.tensor_mul(s4, s4, g))
        em.ins("vector", lambda e, b4=b4, m=m, s4=s4: e.tensor_mul(b4, m, s4))
        em.ins("vector", lambda e, b4=b4, b=b: e.tensor_sub(b4, b, b4))
        em.ins("vector", lambda e, b4=b4: e.tensor_scalar(
            b4, b4, 4.0, -4.0, ALU.mult, ALU.add))
        em.ins("vector", lambda e, s4=s4: e.tensor_scalar_mul(s4, s4, 4.0), inc="VE")

    pk = min(4, nC)
    se_x1 = 0
    ln1_done_te = 0
    bankrot = [4, 5, 6]
    bi = 0
    prev_evac_se = {4: 0, 5: 0, 6: 0}
    first_b5 = [True]
    x1T3 = xln1T.ap().rearrange("p (c q) -> p c q", q=T)
    te_x_t = [0] * nT
    se_xt = {}
    TPT = TOKC // 128

    def emit_ln1t(t):
        nonlocal bi
        for c0 in range(0, nC, pk):
            cw = min(pk, nC - c0)
            bank = ps[bankrot[bi % 3]]
            bno = bankrot[bi % 3]
            bi += 1
            for j in range(cw):
                em.ins("tensor", lambda e, c0=c0, t=t, j=j, bank=bank: e.transpose(
                    bank[:, j * 128:(j + 1) * 128],
                    xbuf[:, t * D + (c0 + j) * 128:t * D + (c0 + j + 1) * 128],
                    id_f[:, :]),
                    wait=[("VE", ve_ln[t])] +
                         ([("SE", prev_evac_se[bno])] if j == 0 else []) +
                         ([("VE", ve_incols)]
                          if first_b5[0] and bno == 5 and j == 0 else []),
                    inc=("TE" if j == cw - 1 else None))
                if bno == 5:
                    first_b5[0] = False
            te = em.c("TE")
            em.ins("scalar", lambda e, c0=c0, cw=cw, t=t, bank=bank: e.copy(
                x1T3[:, c0:c0 + cw, t * 128:(t + 1) * 128],
                bank[:, 0:cw * 128].rearrange("p (c j) -> p c j", j=128)),
                wait=[("TE", te)], inc="SE")
            prev_evac_se[bno] = em.c("SE")
        te_x_t[t] = em.c("TE")
        if (t + 1) % TPT == 0:
            se_xt[(t + 1) // TPT - 1] = em.c("SE")

    nga = 0
    for g0 in range(0, nT, LG):
        for t in range(g0, g0 + LG):
            xt = xbuf[:, t * D:(t + 1) * D]
            em.ins("vector", lambda e, xt=xt, t=t: e.reduce_sum(
                sucol1[:, t:t + 1], xt, AX.X), wait=[ld_x[t]])
            em.ins("scalar", lambda e, xt=xt, t=t: e.activation(
                sqd[:, 0:D], xt, AF.Square, accum_out=sscol1[:, t:t + 1]),
                wait=[ld_x[t], ("VE", em.c("VE"))], inc="SE")
            se_sq_t[t] = em.c("SE")
        se_sq1 = se_sq_t[g0 + LG - 1]
        sg0 = slice(g0, g0 + LG)
        em.ins("vector", lambda e, sg0=sg0: e.tensor_scalar_mul(
            mcol1[:, sg0], sucol1[:, sg0], 1.0 / D), wait=[("SE", se_sq1)])
        em.ins("vector", lambda e, sg0=sg0: e.tensor_scalar_mul(
            rcol1[:, sg0], sscol1[:, sg0], 1.0 / D))
        em.ins("vector", lambda e, sg0=sg0: e.tensor_mul(
            sucol1[:, sg0], mcol1[:, sg0], mcol1[:, sg0]))
        em.ins("vector", lambda e, sg0=sg0: e.tensor_sub(
            rcol1[:, sg0], rcol1[:, sg0], sucol1[:, sg0]))
        em.ins("vector", lambda e, sg0=sg0: e.tensor_scalar_add(
            rcol1[:, sg0], rcol1[:, sg0], EPS), inc="VE")
        rsqrt_cols(rcol1[:, sg0], rcol1[:, sg0], LG, 0)
        for t in range(g0, g0 + LG):
            xt = xbuf[:, t * D:(t + 1) * D]
            em.ins("vector", lambda e, xt=xt, t=t: e.tensor_scalar(
                xt, xt, mcol1[:, t:t + 1], rcol1[:, t:t + 1],
                ALU.subtract, ALU.mult), inc="VE")
            ve_ln[t] = em.c("VE")
        if nga == 0:
            em.ins("vector", lambda e: e.tensor_copy(
                incols[:, :], ps[5][:, 0:12 * nC]),
                wait=[("TE", te_vec)], inc="VE")
            ve_incols = em.c("VE")
        if nga < 3:
            emit_afcols_unit(nga)
            nga += 1
        for t in range(g0, g0 + LG):
            emit_ln1t(t)
        if g0 == 0:
            emit_wvo_casts(0, nC)
    while nga < 3:
        emit_afcols_unit(nga)
        nga += 1
    ve_afcols = em.c("VE")
    se_x1 = em.c("SE")
    ln1_done_te = em.c("TE")

    emit_wvo_casts(nC, 2 * nC)

    qbuf_te_free = [0]
    row_extra_wait = []

    def emit_proj(nm, jaf, out_kind):
        wsl = wb[wslot[nm]]
        s4 = afcols[:, (2 * jaf) * nC:(2 * jaf + 1) * nC]
        b4 = afcols[:, (2 * jaf + 1) * nC:(2 * jaf + 2) * nC]
        dst = vtb if out_kind == "vt" else qbuf
        evac_se = [0] * nTC
        if out_kind == "vt":
            vto, _ = L.range_of("vtb")
            xo, xe = L.range_of("xbuf")
        for c in range(nC):
            for tc in range(nTC):
                bank = ps[tc]
                w0 = []
                if c == 0:
                    if out_kind == "vt":
                        w0 = [("SE", se_xt[tc])]
                        if tc == 0:
                            w0.append(_wv(ve_w[nm]))
                    elif tc == 0:
                        w0 = [_wv(ve_w[nm]), ("SE", se_x1)]
                if c > 0:
                    w0.append(("SE", evac_se[tc]))
                for ci in range(nC):
                    em.ins("tensor", lambda e, wsl=wsl, c=c, ci=ci, tc=tc, bank=bank: e.matmul(
                        bank[:, 0:TOKC],
                        wsl[:, ci * D + c * 128:ci * D + (c + 1) * 128],
                        xln1T[:, ci * T + tc * TOKC:ci * T + (tc + 1) * TOKC],
                        start=(ci == 0), stop=(ci == nC - 1)),
                        wait=(w0 if ci == 0 else []),
                        inc=("TE" if ci == nC - 1 else None))
                te = em.c("TE")
                wE = [("TE", te)]
                if c == 0 and tc == 0:
                    wE.append(("VE", ve_afcols))
                if out_kind == "vt":
                    wb0 = vto + (c * T + tc * TOKC) * 2
                    wb1 = wb0 + TOKC * 2
                    if wb0 < xe and wb1 > xo:
                        t_hi = min(nT - 1, max(0, (min(wb1, xe) - 1 - xo) // (D * 4)))
                        wE.append(("TE", te_x_t[t_hi]))
                if out_kind == "row" and c == 0 and tc == 0:
                    wE.append(("TE", qbuf_te_free[0]))
                    wE.extend(row_extra_wait)
                em.ins("scalar", lambda e, c=c, tc=tc, bank=bank, s4=s4, b4=b4, dst=dst:
                       e.activation(
                           dst[:, c * T + tc * TOKC:c * T + (tc + 1) * TOKC],
                           bank[:, 0:TOKC], AF.Sigmoid,
                           bias=b4[:, c:c + 1], scale=s4[:, c:c + 1]),
                       wait=wE, inc="SE")
                evac_se[tc] = em.c("SE")
            ffnw_tick()
        if out_kind == "row":
            row = rowpool[0:1, 0:T] if nm == "q" else rowpool[32:33, 0:T]
            ve_row = 0
            for tc in range(nTC):
                for c in range(nC):
                    em.ins("tensor", lambda e, c=c, tc=tc: e.matmul(
                        ps[4][0:1, 0:TOKC], ones_cb[:, :],
                        qbuf[:, c * T + tc * TOKC:c * T + (tc + 1) * TOKC],
                        start=(c == 0), stop=(c == nC - 1)),
                        wait=(([("SE", evac_se[tc])] if c == 0 else []) +
                              ([("VE", ve_row)] if c == 0 and tc > 0 else [])),
                        inc=("TE" if c == nC - 1 else None))
                tte = em.c("TE")
                em.ins("vector", lambda e, tc=tc, row=row: e.tensor_scalar(
                    row[:, tc * TOKC:(tc + 1) * TOKC], ps[4][0:1, 0:TOKC],
                    -1.0 / D, 1.0, ALU.mult, ALU.add),
                    wait=[("TE", tte)], inc="VE")
                ve_row = em.c("VE")
            qbuf_te_free[0] = em.c("TE")

    use_kstage = (T >= 2 * D)
    if use_kstage:
        kstage = qbuf.bitcast(F32)
        ld_kst = 0
        for c in range(nC):
            ld_kst = dma("scalar", "WA1", kstage[:, c * D:(c + 1) * D],
                         w_h["k"].ap()[c * 128:(c + 1) * 128, :],
                         wait=([("SE", se_x1)] if c == 0 else []))

    emit_proj("v", 2, "vt")
    se_vt_all = em.c("SE")

    slot0_free = em.c("TE")
    wsq = w_h["q"].ap().rearrange("(c p) d -> c p d", p=128)
    for c in range(nC):
        sl = c % 3
        ld = dma("scalar", ("WA0", "WA2", "WA3")[sl], wlda[sl][:, 0:D], wsq[c],
                 wait=[("VE", wlda_ve[sl])])
        em.ins("vector", lambda e, c=c, sl=sl: e.tensor_copy(
            wb[0][:, c * D:(c + 1) * D], wlda[sl][:, 0:D]),
            wait=[ld, ("TE", slot0_free)] +
                 ([("SE", se_x1)] if c == 0 else []), inc="VE")
        wlda_ve[sl] = em.c("VE")
    ve_w["q"] = em.c("VE")

    st_vw = [0] * nT
    ve_vw = [0] * nT
    for t in range(nT):
        par = t % 2
        for h in range(nOC):
            bank = ps[(t % 2) * 2 + h]
            w0 = []
            if t == 0 and h == 0:
                w0 = [("SE", se_vt_all), ("VE", ve_w["o"])]
            if t >= 2:
                w0.append(ve_vw[t - 2])
            for cd in range(nC):
                em.ins("tensor", lambda e, t=t, h=h, cd=cd, bank=bank: e.matmul(
                    bank[:, 0:OC],
                    vtb[:, cd * T + t * 128:cd * T + (t + 1) * 128],
                    wb[wslot["o"]][:, cd * D + h * OC:cd * D + (h + 1) * OC],
                    start=(cd == 0), stop=(cd == nC - 1)),
                    wait=(w0 if cd == 0 else []),
                    inc=("TE" if cd == nC - 1 else None))
            te = em.c("TE")
            em.ins("scalar", lambda e, par=par, h=h, bank=bank: e.copy(
                vwring[par][:, h * OC:(h + 1) * OC], bank[:, 0:OC]),
                wait=[("TE", te)] + ([st_vw[t - 2]] if t >= 2 and h == 0 else []),
                inc="SE")
        ve_vw[t] = ("SE", em.c("SE"))
        st_vw[t] = dma("scalar", f"VS{par}", vw_in.ap()[t * 128:(t + 1) * 128, :],
                       vwring[par][:, 0:D], wait=[ve_vw[t]])
        for h in range(nOC):
            em.ins("tensor", lambda e, par=par, h=h, t=t: e.matmul(
                ps[5 + h][0:1, 0:OC], ones_cb[:, :],
                vwring[par][:, h * OC:(h + 1) * OC],
                start=(t == 0), stop=(t == nT - 1)),
                wait=([ve_vw[t]] if h == 0 else []),
                inc=("TE" if h == nOC - 1 else None))
        ffnw_tick()
    te_cs_all = em.c("TE")
    em.ins("vector", lambda e: e.tensor_copy(sqd[0:1, 0:OC], ps[5][0:1, 0:OC]),
           wait=[("TE", te_cs_all)])
    if nOC > 1:
        em.ins("vector", lambda e: e.tensor_copy(sqd[0:1, OC:D], ps[6][0:1, 0:OC]))
    em.ins("vector", lambda e: e.engine_nop(), inc="VE")
    ve_cse = em.c("VE")
    st_cs = dma("scalar", "CSS", cs_in.ap().rearrange("(a b) -> a b", a=1),
                sqd[0:1, 0:D], wait=[("VE", ve_cse)])
    em.ins("gpsimd", lambda e: e.collective_compute(
        "AllGather", ALU.bypass, replica_groups=rg_batch,
        ins=[vw_in.ap().opt()], outs=[vw_out.ap().opt()]),
        wait=[st_vw[nT - 2], st_vw[nT - 1]], inc="CC1")
    cc_vw = ("CC1", em.c("CC1"))
    em.ins("gpsimd", lambda e: e.collective_compute(
        "AllReduce", ALU.add, replica_groups=rg_batch,
        ins=[cs_in.ap().opt()], outs=[cs_out.ap().opt()]),
        wait=[st_cs], inc="CC4")
    cc_cs = ("CC4", em.c("CC4"))

    slot_ve[0] = em.c("TE")
    slot_ve[1] = em.c("TE")
    row_extra_wait.extend([("VS0", em.c("VS0")), ("VS1", em.c("VS1"))])
    if use_kstage:
        for c in range(nC):
            em.ins("scalar", lambda e, c=c: e.copy(
                wb[1][:, c * D:(c + 1) * D], kstage[:, c * D:(c + 1) * D]),
                wait=[ld_kst, ("TE", slot_ve[1])], inc="SE")
        ve_w["k"] = ("SE", em.c("SE"))
    else:
        conv_attn_w("k")
    emit_proj("k", 1, "row")
    ve_tk = em.c("VE")
    st_tk = dma("scalar", "TKS", tk_in.ap().rearrange("(a b) -> a b", a=1),
                rowpool[32:33, 0:T], wait=[("VE", ve_tk)])
    em.ins("gpsimd", lambda e: e.collective_compute(
        "AllGather", ALU.bypass, replica_groups=rg_batch,
        ins=[tk_in.ap().opt()], outs=[tk_out.ap().opt()]),
        wait=[st_tk], inc="CC0")
    cc_tk = ("CC0", em.c("CC0"))
    emit_proj("q", 0, "row")
    ve_tq = em.c("VE")
    ffnw_tick(len(ffnw_units))
    st_spill_all = [("WS0", em.c("WS0")), ("WS1", em.c("WS1"))]
    ld_tk = dma("scalar", "TKL", tkld[0:nK, 0:128],
                tk_out.ap().rearrange("(a b) -> a b", b=128),
                wait=[cc_tk])

    em.barrier()

    em.ins("tensor", lambda e: e.transpose(
        ps[5][0:128, 0:nK], tkld[0:nK, 0:128], id_f[0:nK, 0:nK]),
        wait=[ld_tk], inc="TE")
    te_tk = em.c("TE")
    em.ins("vector", lambda e: e.tensor_copy(tK_sb[:, :], ps[5][0:128, 0:nK]),
           wait=[("TE", te_tk)])
    em.ins("vector", lambda e: e.tensor_scalar_mul(tK2p[:, :], tK_sb[:, :], 2.0))
    em.ins("vector", lambda e: e.tensor_scalar_mul(tK2n[:, :], tK_sb[:, :], -2.0),
           inc="VE")
    ve_tkt = em.c("VE")
    ve_tqc = {}
    for tc in range(nTC):
        bank = ps[4 + (tc % 2)]
        w = [("VE", ve_tkt)] if tc == 1 else []
        if tc >= 2:
            w.append(("VE", ve_tqc[tc - 2]))
        em.ins("tensor", lambda e, tc=tc, bank=bank: e.matmul(
            bank[:, 0:TOKC], ones_rf[:, :], rowpool[0:1, tc * TOKC:(tc + 1) * TOKC],
            start=True, stop=True), wait=w, inc="TE")
        te = em.c("TE")
        em.ins("vector", lambda e, tc=tc, bank=bank: e.tensor_copy(
            tQb[:, tc * TOKC:(tc + 1) * TOKC], bank[:, 0:TOKC]),
            wait=[("TE", te)], inc="VE")
        ve_tqc[tc] = em.c("VE")
    ve_tqb = em.c("VE")
    ld_bo = dma("sync", "VEC", borow[:, :], bo_h.ap().rearrange("(a b) -> a b", a=1),
                wait=st_spill_all)
    ld_cs = dma("sync", "CSL", crow[:, :],
                cs_out.ap().rearrange("(a b) -> a b", a=1),
                wait=[cc_cs] + st_spill_all)
    em.ins("vector", lambda e: e.scalar_tensor_tensor(
        crow[:, 0:D], borow[:, 0:D], 1.0 / 0.9, crow[:, 0:D],
        ALU.mult, ALU.add), wait=[ld_cs, ld_bo], inc="VE")
    ve_crow = em.c("VE")

    vws = vw_out.ap().rearrange("(kt p) d -> kt p d", p=128)
    kstep = max(1, nK // 8)
    ld_vw_c = {}
    for ci, k0 in enumerate(range(0, nK, kstep)):
        k1 = min(nK, k0 + kstep)
        v = dma("sync", f"VWL{ci}",
                vw_sb[:, k0 * D:k1 * D].rearrange("p (k d) -> p k d", d=D),
                vw_out.ap()[k0 * 128:k1 * 128, :].rearrange(
                    "(k p) d -> p k d", p=128),
                wait=(([cc_vw] + st_spill_all) if k0 == 0 else []))
        for k in range(k0, k1):
            ld_vw_c[k] = v
    ld_vw = v

    sucol = stat[:, 0:nT]
    sscol = stat[:, nT:2 * nT]
    mcol = stat[:, 3 * nT:4 * nT]
    rcol = stat[:, 4 * nT:5 * nT]
    amcol = stat[:, 5 * nT:6 * nT]
    r2i = stat[:, 6 * nT:7 * nT]
    amr = stat[:, 7 * nT:7 * nT + 1]
    NB = nT - QG if nT > QG else 0
    te_av = {}
    ve_d = {}
    ge_d = {}
    se_d = {}
    ve_evac = [0] * nT
    ve_banks = [0] * nT
    st_xmid = [0] * nT
    pend_stats = []
    batchA_q = []
    def emit_dbuild(j):
        qg_, kt_ = divmod(j, nK)
        par = j % 3
        tkp = tK2p[:, kt_:kt_ + 1]
        tkn = tK2n[:, kt_:kt_ + 1]
        tkc = tK_sb[:, kt_:kt_ + 1]
        tqs = tQb[:, qg_ * QG * 128:(qg_ + 1) * QG * 128]
        wS = [("VE", ve_d[j - 3])] if j >= 3 else [("VE", ve_tqb), ("VE", ve_tkt)]
        em.ins("scalar", lambda e, par=par, tqs=tqs, tkp=tkp: e.activation(
            ub[par][:, :], tqs, AF.Exp, bias=tkp, scale=-2.0), wait=wS)
        em.ins("scalar", lambda e, par=par, tqs=tqs, tkn=tkn: e.activation(
            wub[par][:, :], tqs, AF.Exp, bias=tkn, scale=2.0), inc="SE")
        se_d[j] = em.c("SE")
        em.ins("vector", lambda e, par=par: e.tensor_scalar_mul(
            db[par][:, :], ub[par][:, :], -1.0),
            wait=[("SE", se_d[j])] + ([("TE", te_av[j - 3])] if j >= 3 else []))
        em.ins("vector", lambda e, par=par, tqs=tqs, tkc=tkc: e.tensor_single_scalar(
            mb[par][:, :], tqs, tkc, ALU.is_lt))
        em.ins("vector", lambda e, par=par: e.copy_predicated(
            db[par][:, :], mb[par][:, :], wub[par][:, :]), inc="VE")
        ve_d[j] = em.c("VE")

    for qg in range(nQG):
        for qj in range(QG):
            for h in range(nOC):
                bank = ps[qj * nOC + h]
                w1 = [("VE", ve_crow)] if qg == 0 and qj == 0 and h == 0 else []
                if qg > 0:
                    w1.append(("VE", ve_banks[(qg - 1) * QG + qj]))
                em.ins("tensor", lambda e, qj=qj, h=h, bank=bank: e.matmul(
                    bank[:, 0:OC], ones_rf[:, :], crow[:, h * OC:(h + 1) * OC],
                    start=True, stop=False), wait=w1,
                    inc=("TE" if (qj == QG - 1 and h == nOC - 1) else None))
        for kt in range(nK):
            j = qg * nK + kt
            par = j % 3
            if j not in ve_d:
                emit_dbuild(j)
            if kt == 4 and pend_stats:
                pend_stats.pop(0)()
                if qg == nQG - 1 and NB > 0:
                    batchA_q.extend([
                        lambda: em.ins("vector", lambda e: e.tensor_scalar(
                            rcol[:, 0:NB], sscol[:, 0:NB], 1.0 / D, EPS,
                            ALU.mult, ALU.add),
                            wait=[("VE", ve_evac[NB - 1])]),
                        lambda: em.ins("vector", lambda e: e.reciprocal(
                            r2i[:, 0:NB], rcol[:, 0:NB])),
                        lambda: em.ins("vector", lambda e: e.tensor_mul(
                            amcol[:, 0:NB], amcol[:, 0:NB], r2i[:, 0:NB]),
                            inc="VE"),
                        lambda: rsqrt_cols(rcol[:, 0:NB], rcol[:, 0:NB], NB, 0),
                    ])
            elif batchA_q:
                batchA_q.pop(0)()
            w0 = [("VE", ve_d[j])]
            if qg == 0 and kt % kstep == 0:
                w0.append(ld_vw_c[kt])
            for qj in range(QG):
                for h in range(nOC):
                    bank = ps[qj * nOC + h]
                    last = (qj == QG - 1 and h == nOC - 1)
                    em.ins("tensor", lambda e, par=par, qj=qj, h=h, kt=kt, bank=bank:
                           e.matmul(
                               bank[:, 0:OC],
                               db[par][:, qj * 128:(qj + 1) * 128],
                               vw_sb[:, kt * D + h * OC:kt * D + (h + 1) * OC],
                               start=False, stop=(kt == nK - 1)),
                           wait=(w0 if (qj == 0 and h == 0) else []),
                           inc=("TE" if last else None))
            te_av[j] = em.c("TE")
        te_qg = em.c("TE")
        for qj in range(QG):
            t = qg * QG + qj
            par = t % 4
            ldx = dma("sync", f"XR{t % 2}", xrel[t % 2][:, 0:D], xs[t],
                      wait=[("VE", ve_evac[t - 2])] if t >= 2 else [])
            for h in range(nOC):
                bank = ps[qj * nOC + h]
                acc = tmpc[:, 8 + 2 * qj + h:9 + 2 * qj + h]
                em.ins("vector", lambda e, par=par, t=t, h=h, bank=bank, acc=acc:
                       e.scalar_tensor_tensor(
                           xmr[par][:, h * OC:(h + 1) * OC], bank[:, 0:OC], 0.9,
                           xrel[t % 2][:, h * OC:(h + 1) * OC], ALU.mult, ALU.add,
                           accum_out=acc),
                       wait=[("TE", te_qg), ldx] +
                            ([st_xmid[t - 4], ("VE", ve_evac[t - 4])]
                             if t >= 4 else []))
            em.ins("vector", lambda e: e.engine_nop(), inc="VE")
            ve_banks[t] = em.c("VE")
            st_xmid[t] = dma("scalar", "XM2" if t >= nT - QG else f"XM{t % 2}",
                             xmid_h.ap()[t * 128:(t + 1) * 128, :],
                             xmr[par][:, 0:D], wait=[("VE", ve_banks[t])])
            if qg + 1 < nQG and qj < min(3, nK):
                emit_dbuild((qg + 1) * nK + qj)

        def mk_stats(qg):
            def emit():
                for qj in range(QG):
                    t = qg * QG + qj
                    par = t % 4
                    su = stat[:, 0 * nT + t:0 * nT + t + 1]
                    ssq = stat[:, 1 * nT + t:1 * nT + t + 1]
                    mcol = stat[:, 3 * nT + t:3 * nT + t + 1]
                    am = stat[:, 5 * nT + t:5 * nT + t + 1]
                    a0 = tmpc[:, 8 + 2 * qj:9 + 2 * qj]
                    a1 = tmpc[:, 9 + 2 * qj:10 + 2 * qj] if nOC > 1 else None
                    if nOC > 1:
                        em.ins("vector", lambda e, su=su, a0=a0, a1=a1: e.tensor_add(
                            su, a0, a1), wait=[("VE", ve_banks[t])])
                    else:
                        em.ins("vector", lambda e, su=su, a0=a0: e.tensor_copy(
                            su, a0), wait=[("VE", ve_banks[t])])
                    em.ins("vector", lambda e, su=su, mcol=mcol: e.tensor_scalar_mul(
                        mcol, su, -1.0 / D), inc="VE")
                    ve_m = em.c("VE")
                    em.ins("scalar", lambda e, par=par, ssq=ssq, mcol=mcol:
                           e.activation(
                               sqd[:, 0:D], xmr[par][:, 0:D], AF.Square,
                               bias=mcol, accum_out=ssq),
                        wait=[("VE", ve_m)], inc="SE")
                    se_sq = em.c("SE")
                    em.ins("vector", lambda e, am=am: e.reduce_max(
                        am, sqd[:, 0:D], AX.X), wait=[("SE", se_sq)], inc="VE")
                    ve_evac[t] = em.c("VE")
            return emit
        pend_stats.append(mk_stats(qg))

    te_avl = em.c("TE")
    while batchA_q:
        batchA_q.pop(0)()

    def armax_pre(idx, ve_ready):
        em.ins("tensor", lambda e: e.transpose(ps[6][0:1, 0:128], amr, id_f[:, :]),
               wait=[("VE", ve_ready)], inc="TE")
        te = em.c("TE")
        em.ins("vector", lambda e: e.reduce_max(p0ex[:, 2:3], ps[6][0:1, 0:128], AX.X),
               wait=[("TE", te)])
        for i in range(8):
            em.ins("vector", lambda e, i=i: e.tensor_copy(
                p0ex[:, 8 + i:9 + i], p0ex[:, 2:3]))
        em.ins("vector", lambda e: e.engine_nop(), inc="VE")
        ve = em.c("VE")
        st = dma("scalar", f"MXS{idx}", mx_in[idx].ap().rearrange("(a b) -> a b", a=1),
                 p0ex[:, 8:16], wait=[("VE", ve)])
        ccs = "CC2" if idx == 0 else "CC3"
        em.ins("gpsimd", lambda e, idx=idx: e.collective_compute(
            "AllReduce", ALU.max, replica_groups=rg_all,
            ins=[mx_in[idx].ap().opt()], outs=[mx_out[idx].ap().opt()]),
            wait=[st], inc=ccs)
        return (ccs, em.c(ccs))

    def armax_post(idx, clip_cell, cc, sq=False):
        ld = dma("sync", f"MXL{idx}", p0ex[:, 16:24],
                 mx_out[idx].ap().rearrange("(a b) -> a b", a=1), wait=[cc])
        em.ins("scalar", lambda e, clip_cell=clip_cell: e.activation(
            p0ex[:, 3:4], p0ex[:, clip_cell:clip_cell + 1], AF.Sigmoid),
            wait=[ld, ld_sc], inc="SE")
        gsrc = 16
        if sq:
            em.ins("scalar", lambda e: e.activation(
                p0ex[:, 6:7], p0ex[:, 16:17], AF.Sqrt), inc="SE")
            gsrc = 6
        se = em.c("SE")
        em.ins("vector", lambda e, gsrc=gsrc: e.tensor_mul(
            p0ex[:, 4:5], p0ex[:, 3:4], p0ex[:, gsrc:gsrc + 1]), wait=[("SE", se)])
        em.ins("vector", lambda e: e.tensor_scalar_mul(p0ex[:, 4:5], p0ex[:, 4:5],
                                                       1.0 / QMAX))
        em.ins("vector", lambda e: e.reciprocal(p0ex[:, 5:6], p0ex[:, 4:5]), inc="VE")
        ve2 = em.c("VE")
        for z, (src_, dc) in enumerate(((5, 2 * idx), (4, 2 * idx + 1))):
            em.ins("tensor", lambda e, src_=src_, dc=dc: e.matmul(
                ps[6][:, 128 + dc:129 + dc], ones_rf[:, :], p0ex[:, src_:src_ + 1],
                start=True, stop=True),
                wait=([("VE", ve2)] if z == 0 else []),
                inc=("TE" if z == 1 else None))
        te2 = em.c("TE")
        em.ins("vector", lambda e, idx=idx: e.tensor_copy(
            ccol[:, 2 * idx:2 * idx + 2], ps[6][:, 128 + 2 * idx:130 + 2 * idx]),
            wait=[("TE", te2)], inc="VE")
        return em.c("VE")

    xms = xmid_h.ap().rearrange("(t p) d -> t p d", p=128)
    rxT3 = rxT.ap().rearrange("p (c t) -> p c t", t=T) if nC > 1 else None
    ve_ap2 = [0] * nT
    rx_te = [0] * nT
    rx_bank_se = [0, 0]

    def _ovl(a, b):
        (a0, a1), (b0, b1) = L.range_of(a), L.range_of(b)
        return a0 < b1 and b0 < a1

    rxb_xmr = any(_ovl(f"rxb{i}", f"xmr{j}")
                  for i in range(2) for j in range(4))

    def emit_xln2t(t, pre=False):
        par = t % 2
        xm_w = ([st_xmid[nT - QG - 2], st_xmid[nT - QG - 1]]
                if (NB > 0 and t < NB) else [st_xmid[nT - 1]])
        ldx = dma("sync", f"X2{par}", xrel2[par][:, 0:D], xms[t],
                  wait=xm_w +
                       ([("TE", te_avl), ("VE", ve_crow)] if pre else []) +
                       ([("VE", ve_ap2[t - 2])] if t >= 2 else []))
        mc = stat[:, 3 * nT + t:3 * nT + t + 1]
        rc = stat[:, 4 * nT + t:4 * nT + t + 1]
        rb = rxb[par]
        em.ins("vector", lambda e, par=par, mc=mc, rc=rc, rb=rb: e.tensor_scalar(
            rb[:, 0:D], xrel2[par][:, 0:D], mc, rc, ALU.add, ALU.mult),
            wait=[ldx] +
                 ([("TE", rx_te[t - 2])] if t >= 2
                  else [("TE", te_avl)] +
                       ([st_xmid[nT - 1]] if rxb_xmr else [])) +
                 (st_spill_all if t == 0 else []), inc="VE")
        ve_ap2[t] = em.c("VE")
        for c0 in range(0, nC, 4):
            cw = min(4, nC - c0)
            bno = (c0 // 4) % 2
            bank = ps[4 + bno].bitcast(BF16)
            wj = [("VE", ve_ap2[t]), ("SE", rx_bank_se[bno])]
            if pre and t < 2 and 4 + bno < QG * nOC:
                wj.append(("VE", ve_banks[(nQG - 1) * QG + (4 + bno) // nOC]))
            for j in range(cw):
                em.ins("tensor", lambda e, rb=rb, c0=c0, j=j, bank=bank: e.transpose(
                    bank[:, j * 128:(j + 1) * 128],
                    rb[:, (c0 + j) * 128:(c0 + j + 1) * 128], id_b[:, :]),
                    wait=(wj if j == 0 else []),
                    inc=("TE" if j == cw - 1 else None))
            te = em.c("TE")
            if nC > 1:
                dst = rxT3[:, c0:c0 + cw, t * 128:(t + 1) * 128]
                srcb = bank[:, 0:cw * 128].rearrange("p (c j) -> p c j", j=128)
            else:
                dst = rxT[:, t * 128:(t + 1) * 128]
                srcb = bank[:, 0:128]
            em.ins("scalar", lambda e, dst=dst, srcb=srcb: e.copy(dst, srcb),
                   wait=[("TE", te)], inc="SE")
            rx_bank_se[bno] = em.c("SE")
        rx_te[t] = em.c("TE")

    xl2_early = NB >= 2 and not rxb_xmr
    if xl2_early:
        emit_xln2t(0, pre=True)
        emit_xln2t(1, pre=True)
    while pend_stats:
        pend_stats.pop(0)()
    em.ins("vector", lambda e: e.tensor_scalar(
        rcol[:, NB:nT], sscol[:, NB:nT], 1.0 / D, EPS, ALU.mult, ALU.add),
        wait=[("VE", ve_evac[nT - 1])])
    em.ins("vector", lambda e: e.reciprocal(r2i[:, NB:nT], rcol[:, NB:nT]))
    em.ins("vector", lambda e: e.tensor_mul(
        amcol[:, NB:nT], amcol[:, NB:nT], r2i[:, NB:nT]))
    em.ins("vector", lambda e: e.reduce_max(amr, amcol, AX.X), inc="VE")
    ve_am = em.c("VE")
    cc_ar1 = armax_pre(0, ve_am)
    rsqrt_cols(rcol[:, NB:nT], rcol[:, NB:nT], nT - NB, 0)
    if not xl2_early:
        emit_xln2t(0, pre=True)
        if nT > 1:
            emit_xln2t(1, pre=True)
    for t in range(2, nT):
        emit_xln2t(t, pre=True)
    se_rx_all = em.c("SE")

    em.barrier()

    ve_s1 = armax_post(0, 0, cc_ar1, sq=True)
    em.ins("vector", lambda e: e.tensor_scalar_mul(cgc[:, :], rmg[:, :], ccol[:, 1:2]))
    em.ins("vector", lambda e: e.tensor_scalar_mul(cgc[:, :], cgc[:, :], 1.0 / QMAX))
    em.ins("vector", lambda e: e.tensor_scalar_mul(cuc[:, :], rmu[:, :], ccol[:, 1:2]))
    em.ins("vector", lambda e: e.tensor_scalar_mul(cuc[:, :], cuc[:, :], 1.0 / QMAX),
           inc="VE")
    ve_c1 = em.c("VE")

    HQ2 = min(1024, T)
    qsl = [sqd[:, 0:HQ2], xrel2[0][:, 0:HQ2]]
    ve_rx_hc = [0] * nC
    ve_qtok = []
    ki = 0
    for hc in range(nC):
        for q0 in range(0, T, HQ2):
            sl = rxT[:, hc * T + q0:hc * T + q0 + HQ2]
            qs = qsl[ki % 2]
            em.ins("scalar", lambda e, sl=sl, qs=qs: e.activation(
                qs, sl, AF.Copy, bias=RND, scale=ccol[:, 0:1]),
                wait=([("VE", ve_s1)] if ki == 0 else []) +
                     ([("VE", ve_qtok[ki - 2])] if ki >= 2 else []), inc="SE")
            se_q = em.c("SE")
            em.ins("vector", lambda e, qs=qs: e.tensor_scalar(
                qs, qs, RND, -QMAX - 1.0, ALU.subtract, ALU.max),
                wait=[("SE", se_q)])
            em.ins("vector", lambda e, sl=sl, qs=qs: e.tensor_single_scalar(
                sl, qs, QMAX, ALU.min), inc="VE")
            ve_qtok.append(em.c("VE"))
            ki += 1
        ve_rx_hc[hc] = em.c("VE")
    ve_rx = em.c("VE")

    ld_wff = [0] * nI
    ve_h = [0] * nI
    se_h = [0] * nI
    te_gu = [0] * nI
    hamax = stat[:, 2 * nT:2 * nT + 1]
    hsc = stat[:, 2 * nT + 1:2 * nT + 2]
    se_silu = [[0] * nTC for _ in range(nI)]
    ve_hstt = [[0] * nTC for _ in range(nI)]
    for it in range(nI):
        par = it % 2
        w_g = wff[par][:, 0:nC * 128]
        w_u = wff[par][:, nC * 128:2 * nC * 128]
        wld = list(st_spill_all)
        if it >= 2:
            wld += [("TE", te_gu[it - 2])]
        dma("sync", f"WG{par}", w_g.rearrange("p (c j) -> p c j", j=128),
            wT_sp["g"].ap()[it].rearrange("c p j -> p c j"), wait=wld)
        ld_wff[it] = dma("sync", f"WG{par}", w_u.rearrange("p (c j) -> p c j", j=128),
                         wT_sp["u"].ap()[it].rearrange("c p j -> p c j"))
        for tc in range(nTC):
            for z, wsl in ((0, w_g), (1, w_u)):
                bank = ps[z * 4 + tc]
                for hc in range(nC):
                    w0 = []
                    if it == 0:
                        w0.append(("VE", ve_rx_hc[hc]))
                    if hc == 0:
                        if z == 0 and tc == 0:
                            w0 = w0 + [ld_wff[it]]
                        if it >= 1:
                            w0.append(("SE", se_silu[it - 1][tc]) if z == 0
                                      else ("VE", ve_hstt[it - 1][tc]))
                    em.ins("tensor", lambda e, wsl=wsl, hc=hc, tc=tc, bank=bank:
                           e.matmul(
                               bank[:, 0:TOKC], wsl[:, hc * 128:(hc + 1) * 128],
                               rxT[:, hc * T + tc * TOKC:hc * T + (tc + 1) * TOKC],
                               start=(hc == 0), stop=(hc == nC - 1)),
                           wait=w0,
                           inc=("TE" if hc == nC - 1 else None))
                if z == 0:
                    te_g = em.c("TE")
                else:
                    te_u = em.c("TE")
            em.ins("scalar", lambda e, it=it, tc=tc: e.activation(
                sg[:, tc * TOKC:(tc + 1) * TOKC], ps[tc][:, 0:TOKC], AF.Silu,
                scale=cgc[:, it:it + 1]),
                wait=[("TE", te_g)] +
                     ([("VE", ve_hstt[it - 1][tc])] if it >= 1 else []), inc="SE")
            se_silu[it][tc] = em.c("SE")
            em.ins("vector", lambda e, it=it, tc=tc: e.scalar_tensor_tensor(
                h_sb[:, it * T + tc * TOKC:it * T + (tc + 1) * TOKC],
                ps[4 + tc][:, 0:TOKC], cuc[:, it:it + 1],
                sg[:, tc * TOKC:(tc + 1) * TOKC], ALU.mult, ALU.mult),
                wait=[("SE", se_silu[it][tc]), ("TE", te_u)])
            em.ins("vector", lambda e, it=it, tc=tc: e.reduce_max(
                hsc, h_sb[:, it * T + tc * TOKC:it * T + (tc + 1) * TOKC],
                AX.X, apply_absolute_value=True))
            if it == 0 and tc == 0:
                em.ins("vector", lambda e: e.tensor_copy(hamax, hsc), inc="VE")
            else:
                em.ins("vector", lambda e: e.tensor_max(hamax, hamax, hsc), inc="VE")
            ve_hstt[it][tc] = em.c("VE")
        te_gu[it] = em.c("TE")
        ve_h[it] = em.c("VE")
        se_h[it] = em.c("SE")
    ve_h_all = em.c("VE")
    se_h_all = em.c("SE")
    te_gu_all = em.c("TE")

    em.ins("vector", lambda e: e.tensor_copy(amr, hamax), wait=[("VE", ve_h_all)],
           inc="VE")
    cc_ar2 = armax_pre(1, em.c("VE"))

    ld_wdt = [0] * nI
    for ic in range(nI):
        ld_wdt[ic] = dma("sync", "WDL",
                         wdT_sb[:, ic * D:(ic + 1) * D].rearrange(
                             "p (c j) -> p c j", j=128),
                         wT_sp["d"].ap()[ic].rearrange("c p j -> p c j"),
                         wait=(st_spill_all + [("SE", se_h_all),
                               ("VE", ve_h_all), ("TE", te_gu_all)]
                               if ic == 0 else []))

    ve_s2 = armax_post(1, 1, cc_ar2)

    ve_rh = [0] * nI
    HQ = min(1024, T)
    for it in range(nI):
        for q0 in range(0, T, HQ):
            em.ins("scalar", lambda e, it=it, q0=q0: e.activation(
                vd2[:, 0:HQ], h_sb[:, it * T + q0:it * T + q0 + HQ], AF.Copy,
                bias=RND, scale=ccol[:, 2:3]),
                wait=[("VE", ve_s2)] + ([("VE", ve_rh[it - 1])] if q0 == 0 and it > 0
                                        else []),
                inc="SE")
            se = em.c("SE")
            em.ins("vector", lambda e: e.tensor_scalar(
                vd2[:, 0:HQ], vd2[:, 0:HQ], RND, -QMAX - 1.0, ALU.subtract, ALU.max),
                wait=[("SE", se)])
            em.ins("vector", lambda e, it=it, q0=q0: e.tensor_single_scalar(
                h_sb[:, it * T + q0:it * T + q0 + HQ], vd2[:, 0:HQ], QMAX, ALU.min),
                inc="VE")
        ve_rh[it] = em.c("VE")

    ve_y = [0] * nT
    st_y = [0] * nT
    for tt in range(nT):
        par = tt % 2
        ldx = dma("sync", f"X2{par}", xrel2[par][:, 0:D], xms[tt],
                  wait=[("VE", ve_rx)] +
                       ([("VE", ve_y[tt - 2])] if tt >= 2 else []))
        for h in range(nOC):
            bank = ps[(tt % 3) * 2 + h]
            for ic in range(nI):
                w0 = []
                if tt == 0:
                    w0 = [("VE", ve_rh[ic]), ld_wdt[nI - 1]]
                    if ic == 0 and h == 0:
                        w0.append(("SE", se_h_all))
                elif ic == 0 and h == 0 and tt >= 3:
                    w0 = [("VE", ve_y[tt - 3])]
                em.ins("tensor", lambda e, tt=tt, h=h, ic=ic, bank=bank: e.matmul(
                    bank[:, 0:OC],
                    h_sb[:, ic * T + tt * 128:ic * T + (tt + 1) * 128],
                    wdT_sb[:, ic * D + h * OC:ic * D + (h + 1) * OC],
                    start=(ic == 0), stop=(ic == nI - 1)),
                    wait=w0, inc=("TE" if ic == nI - 1 else None))
            te = em.c("TE")
            em.ins("vector", lambda e, par=par, h=h, bank=bank: e.scalar_tensor_tensor(
                ybuf[par][:, h * OC:(h + 1) * OC], bank[:, 0:OC], ccol[:, 3:4],
                xrel2[par][:, h * OC:(h + 1) * OC], ALU.mult, ALU.add),
                wait=[("TE", te), ldx] +
                     ([st_y[tt - 2]] if tt >= 2 and h == 0 else []),
                inc=("VE" if h == nOC - 1 else None))
        ve_y[tt] = em.c("VE")
        st_y[tt] = dma("scalar", f"YS{par}", out_h.ap()[tt * 128:(tt + 1) * 128, :],
                       ybuf[par][:, 0:D], wait=[("VE", ve_y[tt])])

    em.wait("scalar", "YS0", em.c("YS0"))
    em.wait("scalar", "YS1", em.c("YS1"))
    em.barrier()

    with ExitStack() as stk:
        em.sems = {s: stk.enter_context(nc.semaphore(s)) for s in SEMS}
        with nc.Block() as block:
            em.replay(block)

    return nc


def kernel(**inputs):
    cfg = full_cfg()
    T, GROUP, CORES = cfg["T"], cfg["GROUP"], cfg["CORES"]
    nc = build_program(cfg)
    x = np.ascontiguousarray(np.asarray(inputs["L0_attn_x"], dtype=np.float32))
    Bsz, N, D = x.shape
    shared = {k: np.ascontiguousarray(np.asarray(v, dtype=np.float32))
              for k, v in inputs.items() if k != "L0_attn_x"}
    in_maps = []
    for core in range(CORES):
        b, q = core // GROUP, core % GROUP
        m = dict(shared)
        m["L0_attn_x"] = np.ascontiguousarray(x[b, q * T:(q + 1) * T, :])
        in_maps.append(m)
    res = run_bass_kernel_spmd(nc, in_maps, core_ids=list(range(CORES)))
    out = np.empty((Bsz, N, D), dtype=np.float32)
    for core in range(CORES):
        b, q = core // GROUP, core % GROUP
        out[b, q * T:(q + 1) * T, :] = res.results[core]["out"]
    return out


# version 38
# speedup vs baseline: 1.0175x; 1.0175x over previous
import numpy as np
from contextlib import ExitStack

import concourse.bass as bass
import concourse.mybir as mybir
from concourse.bass_utils import run_bass_kernel_spmd

F32 = mybir.dt.float32
BF16 = mybir.dt.bfloat16
ALU = mybir.AluOpType
AF = mybir.ActivationFunctionType
AX = mybir.AxisListType

RND = 12582912.0
QMAX = 127.0
EPS = 1e-5

ENGS = ("sync", "scalar", "vector", "tensor", "gpsimd")
SEMS = ("TE", "VE", "SE", "BAR", "GE", "CC0", "CC1", "CC2", "CC3",
        "VEC", "XLD", "XL1", "XL2", "XL3", "WA0", "WA1", "WA2", "WA3", "WF0", "WF1", "TKL", "VWL", "MXL0", "MXL1",
        "XR0", "XR1", "X20", "X21", "WG0", "WG1", "WDL", "WF2", "WF3",
        "VWL0", "VWL1", "VWL2", "VWL3", "VWL4", "VWL5", "VWL6", "VWL7",
        "WV", "WO",
        "VS0", "VS1", "TKS", "XM0", "XM1", "XM2", "MXS0", "MXS1", "WS0", "WS1",
        "YS0", "YS1", "CSS", "CSL", "CC4")


def full_cfg():
    return dict(T=2048, D=1024, I=2816, GROUP=4, CORES=8)


def test_cfg():
    return dict(T=256, D=256, I=384, GROUP=4, CORES=8)


def test2_cfg():
    return dict(T=512, D=512, I=640, GROUP=4, CORES=8, TOKC=256, QG=2)


_ESEM = {"vector": "VE", "scalar": "SE", "tensor": "TE"}


class Em:
    def __init__(self):
        self.streams = {e: [] for e in ENGS}
        self.cnt = {s: 0 for s in SEMS}
        self.last = {"vector": 0, "scalar": 0}
        self.sems = None

    def c(self, s):
        return self.cnt[s]

    def wait(self, e, s, v):
        if v > 0:
            self.streams[e].append(("w", s, v))

    def ins(self, e, fn, wait=(), inc=None, n=1):
        wait = [(s, v) for (s, v) in wait if v > 0]
        if e in ("vector", "scalar") and (inc is None or inc == _ESEM[e]):
            sem = _ESEM[e]
            if self.last[e] > 0:
                wait.append((sem, self.last[e]))
            inc, n = sem, 1
        wm = {}
        for s, v in wait:
            wm[s] = max(wm.get(s, 0), v)
        wait = list(wm.items())
        if inc is not None:
            self.cnt[inc] += n
        self.streams[e].append(("i", fn, inc, n, wait))
        if e in ("vector", "scalar") and inc == _ESEM[e]:
            self.last[e] = self.cnt[inc]
        return self.cnt[inc] if inc is not None else None

    def raw(self, e, fn, wait=()):
        wait = [(s, v) for (s, v) in wait if v > 0]
        self.streams[e].append(("i", fn, None, 0, wait))

    def barrier(self):
        for e in ENGS:
            if e in ("tensor", "vector", "scalar", "gpsimd"):
                self.raw(e, lambda eng: eng.drain())
            self.cnt["BAR"] += 1
            self.raw(e, lambda eng: eng.sem_inc(self.sems["BAR"], 1))
        tgt = self.cnt["BAR"]
        for e in ENGS:
            self.wait(e, "BAR", tgt)

    def replay(self, block):
        for e in ENGS:
            def body(eng, _e=e):
                for item in self.streams[_e]:
                    if item[0] == "w":
                        eng.wait_ge(self.sems[item[1]], item[2])
                    else:
                        _, fn, inc, n, wait = item
                        for s, v in wait[1:]:
                            eng.wait_ge(self.sems[s], v)
                        ins = fn(eng)
                        if wait:
                            s, v = wait[0]
                            ins._wait_ge(self.sems[s], v)
                        if inc is not None:
                            ins.then_inc(self.sems[inc], n)
            getattr(block, e)(body)


class Layout:
    PHASES = ("perm", "A1", "A2", "A3", "B", "C1", "C2", "D")

    def __init__(self, nc, limit=None):
        self.nc = nc
        base = nc._sbuf_addr_for_side("left")
        self.base = (base + 63) & ~63
        avail = nc._sbuf_addr_for_side("right") - self.base - 6144
        self.limit = min(limit, avail) if limit else avail
        nc.alloc_sbuf_tensor("manual_arena", [128, (self.limit + self.base - base)],
                             mybir.dt.uint8)
        self.placed = []

    def at(self, name, shape, dtype, phases=("perm", "D")):
        nbytes = int(np.prod(shape[1:])) * mybir.dt.size(dtype)
        nbytes = (nbytes + 63) & ~63
        p0 = self.PHASES.index(phases[0])
        p1 = self.PHASES.index(phases[1])
        off = 0
        while True:
            clash = None
            for (o, e, q0, q1, _) in self.placed:
                if o < off + nbytes and off < e and not (p1 < q0 or q1 < p0):
                    clash = max(clash or 0, e)
            if clash is None:
                break
            off = (clash + 63) & ~63
        assert off + nbytes <= self.limit, (
            f"SBUF overflow placing {name}: {off}+{nbytes} > {self.limit}")
        self.placed.append((off, off + nbytes, p0, p1, name))
        return self.nc.alloc_sbuf_tensor_at(name, list(shape), dtype,
                                            offset=self.base + off)

    def range_of(self, name):
        for (o, e, _, _, n) in self.placed:
            if n == name:
                return (o, e)
        raise KeyError(name)


def _wv(tok):
    return tok if isinstance(tok, tuple) else ("VE", tok)


def build_program(cfg):
    T, D, I, GROUP, CORES = cfg["T"], cfg["D"], cfg["I"], cfg["GROUP"], cfg["CORES"]
    N = T * GROUP
    nT = T // 128
    nK = N // 128
    nC = D // 128
    OC = min(512, D)
    nOC = D // OC
    TOKC = cfg.get("TOKC", min(512, T))
    nTC = T // TOKC
    nI = I // 128
    QG = cfg.get("QG", min(8 // nOC, nT))
    nQG = nT // QG
    ICH = []
    rem = I
    while rem > 0:
        c = min(1024, rem)
        ICH.append(c)
        rem -= c
    assert nK <= 128 and nTC <= 4 and nC <= 8

    nc = bass.Bass("TRN2", target_bir_lowering=False, debug=False)

    def din(name, shape):
        return nc.dram_tensor(name, list(shape), F32, kind="ExternalInput")

    x_h = din("L0_attn_x", [T, D])
    w_h = {"q": din("L0_attn_Wq", [D, D]), "k": din("L0_attn_Wk", [D, D]),
           "v": din("L0_attn_Wv", [D, D]), "o": din("L0_attn_Wo", [D, D])}
    vec_h = []
    for p in ("q", "k", "v"):
        for f in ("g", "b", "m", "v"):
            vec_h.append(din(f"L0_attn_{f}{p}", [D]))
    bo_h = din("L0_attn_bo", [D])
    clipx_h = din("L0_ffn_clip_x", [1])
    clipd_h = din("L0_ffn_clip_down", [1])
    wg_h = din("L0_ffn_w_gate", [I, D])
    wu_h = din("L0_ffn_w_up", [I, D])
    wd_h = din("L0_ffn_w_down", [D, I])
    out_h = nc.dram_tensor("out", [T, D], F32, kind="ExternalOutput")

    vw_in = nc.dram_tensor("vw_in", [T, D], BF16)
    vw_out = nc.dram_tensor("vw_out", [N, D], BF16)
    tk_in = nc.dram_tensor("tk_in", [T], F32)
    tk_out = nc.dram_tensor("tk_out", [N], F32)
    mx_in = [nc.dram_tensor(f"mx{i}_in", [8], F32) for i in range(2)]
    mx_out = [nc.dram_tensor(f"mx{i}_out", [8], F32) for i in range(2)]
    cs_in = nc.dram_tensor("cs_in", [D], F32)
    cs_out = nc.dram_tensor("cs_out", [D], F32)
    xmid_h = nc.dram_tensor("xmid_h", [T, D], F32)
    wT_sp = {"g": nc.dram_tensor("wgT_sp", [nI, nC, 128, 128], BF16),
             "u": nc.dram_tensor("wuT_sp", [nI, nC, 128, 128], BF16),
             "d": nc.dram_tensor("wdT_sp", [nI, nC, 128, 128], BF16)}

    rg_batch = [list(range(g * GROUP, (g + 1) * GROUP)) for g in range(CORES // GROUP)]
    rg_all = [list(range(CORES))]

    L = Layout(nc)
    PM = ("perm", "D")
    id_f = L.at("id_f", [128, 128], F32, PM)
    id_b = L.at("id_b", [128, 128], BF16, PM)
    ones_cb = L.at("ones_cb", [128, 1], BF16, PM)
    ones_rf = L.at("ones_rf", [1, 128], F32, PM)
    rowpool = L.at("rowpool", [128, max(T, 1024)], F32, PM)
    p0ex = L.at("p0ex", [1, 64], F32, PM)
    crow = L.at("crow", [1, D], F32, ("B", "B"))
    borow = L.at("borow", [1, D], F32, ("B", "B"))
    incols = L.at("incols", [128, 12 * nC], F32, PM)
    afcols = L.at("afcols", [128, 6 * nC], F32, PM)
    NSC = 3 * max(nC, nT) + 8
    tmpc = L.at("tmpc", [128, NSC], F32, PM)
    tK_sb = L.at("tK_sb", [128, nK], F32, PM)
    tK2p = L.at("tK2p", [128, nK], F32, PM)
    tK2n = L.at("tK2n", [128, nK], F32, PM)
    stat = L.at("stat", [128, 8 * nT], F32, PM)
    rmg = L.at("rmg", [128, nI], F32, PM)
    rmu = L.at("rmu", [128, nI], F32, PM)
    rmd = L.at("rmd", [128, nC], F32, PM)
    cgc = L.at("cgc", [128, nI], F32, PM)
    cuc = L.at("cuc", [128, nI], F32, PM)
    ccol = L.at("ccol", [128, 8], F32, PM)
    wldf = [L.at(f"wldf{i}", [128, 1024], F32, ("perm", "A3")) for i in range(4)]
    wintf = [L.at(f"wintf{i}", [128, 1024], BF16, ("perm", "A3")) for i in range(2)]
    wts = [L.at(f"wts{i}", [128, 1024], BF16, ("perm", "A3")) for i in range(2)]
    sqd = L.at("sqd", [128, 1024], F32, PM)

    AA = ("A1", "A3")
    xln1T = L.at("xln1T", [128, nC * T], BF16, AA)
    wb = [L.at(f"wb{i}", [128, nC * D], BF16, AA) for i in range(2)]
    wlda = [L.at(f"wlda{i}", [128, 1024], F32, ("perm", "A3")) for i in range(3)]
    qbuf = L.at("qbuf", [128, nC * T], BF16, ("A2", "A3"))
    vwring = [L.at(f"vwring{i}", [128, D], BF16, ("A2", "A2")) for i in range(2)]
    xbuf = L.at("xbuf", [128, nT * D], F32, ("A1", "A1"))
    vtb = L.at("vtb", [128, nC * T], BF16, ("A2", "A2"))

    BB = ("B", "B")
    tQb = L.at("tQb", [128, T], F32, BB)
    ub = [L.at(f"ub{i}", [128, QG * 128], BF16, BB) for i in range(4)]
    wub = [L.at(f"wub{i}", [128, QG * 128], BF16, BB) for i in range(4)]
    mb = [L.at(f"mb{i}", [128, QG * 128], mybir.dt.uint8, BB) for i in range(4)]
    db = [L.at(f"db{i}", [128, QG * 128], BF16, BB) for i in range(4)]
    xrel = [L.at(f"xrel{i}", [128, D], F32, BB) for i in range(2)]
    xmr = [L.at(f"xmr{i}", [128, D], F32, BB) for i in range(4)]
    vw_sb = L.at("vw_sb", [128, nK * D], BF16, BB)

    xrel2 = [L.at(f"xrel2{i}", [128, D], F32, ("C1", "D")) for i in range(2)]
    ybuf = [L.at(f"ybuf{i}", [128, D], F32, ("C1", "D")) for i in range(2)]
    h_sb = L.at("h_sb", [128, nI * T], BF16, ("C1", "D"))
    wff = [L.at(f"wff{i}", [128, 2 * nC * 128], BF16, ("C1", "C1")) for i in range(2)]
    sg = L.at("sg", [128, T], F32, ("C1", "C1"))
    rxT = L.at("rxT", [128, nC * T], BF16, ("C1", "C1"))
    rxb = [L.at(f"rxb{i}", [128, D], BF16, ("C1", "C1")) for i in range(2)]
    wdT_sb = L.at("wdT_sb", [128, nI * D], BF16, ("C2", "D"))
    vd2 = L.at("vd2", [128, min(1024, T)], F32, ("C2", "D"))
    tkld = L.at("tkld", [128, 128], F32, ("A3", "B"))

    ps = [nc.alloc_psum_tensor(f"ps{i}", [128, 512], F32) for i in range(8)]

    em = Em()

    def dma(q, s, out_ap, in_ap, wait=()):
        v = em.ins(q, lambda eng, o=out_ap, i=in_ap: eng.dma_start(out=o, in_=i),
                   wait=wait, inc=s, n=16)
        return (s, v)

    def dma_g(s, out_ap, in_ap, wait=()):
        return dma("gpsimd", s, out_ap, in_ap, wait=wait)

    def rsqrt_cols(dst, src, n, sc0):
        t0 = tmpc[:, sc0:sc0 + n]
        t1 = tmpc[:, sc0 + n:sc0 + 2 * n]
        t2 = tmpc[:, sc0 + 2 * n:sc0 + 3 * n]
        em.ins("scalar", lambda e: e.activation(t0, src, AF.Sqrt),
               wait=[("VE", em.c("VE"))], inc="SE")
        se = em.c("SE")
        em.ins("vector", lambda e: e.reciprocal(t1, t0), wait=[("SE", se)])
        em.ins("vector", lambda e: e.tensor_mul(t2, t1, t1))
        em.ins("vector", lambda e: e.tensor_mul(t2, t2, src))
        em.ins("vector", lambda e: e.tensor_scalar(t2, t2, -0.5, 1.5, ALU.mult, ALU.add))
        em.ins("vector", lambda e: e.tensor_mul(dst, t1, t2), inc="VE")

    em.ins("gpsimd", lambda e: e.iota(
        wldf[1][:, 512:640], [[-1, 128]], base=0, channel_multiplier=1,
        allow_small_or_imprecise_dtypes=True))
    em.ins("gpsimd", lambda e: e.drain())
    em.cnt["GE"] += 1
    em.ins("gpsimd", lambda e: e.sem_inc(em.sems["GE"], 1))
    cc_iota = em.c("GE")
    em.ins("vector", lambda e: e.tensor_single_scalar(
        id_f[:, :], wldf[1][:, 512:640], 0.0, ALU.is_equal),
        wait=[("GE", cc_iota)], inc="VE")
    em.ins("vector", lambda e: e.tensor_copy(id_b[:, :], id_f[:, :]))
    ve_id = em.c("VE")
    em.ins("vector", lambda e: e.memset(ones_cb[:, :], 1.0))
    em.ins("vector", lambda e: e.memset(ones_rf[:, :], 1.0))

    for k in range(12):
        s, kk = (0, k) if k < 8 else (1, k - 8)
        dma("scalar", "VEC", wldf[s][0:nC, kk * 128:(kk + 1) * 128],
            vec_h[k].ap().rearrange("(a b) -> a b", b=128))
    dma("scalar", "VEC", p0ex[:, 0:1], clipx_h.ap().rearrange("(a b) -> a b", a=1))
    ld_sc = dma("scalar", "VEC", p0ex[:, 1:2],
                clipd_h.ap().rearrange("(a b) -> a b", a=1))

    xs = x_h.ap().rearrange("(t p) d -> t p d", p=128)
    step = max(1, nT // 4)
    ld_x = [0] * nT
    for ci, t0 in enumerate(range(0, nT, step)):
        t1 = min(nT, t0 + step)
        sname = "XLD" if ci == 0 else f"XL{ci}"
        tok = dma("sync", sname,
                  xbuf[:, t0 * D:t1 * D].rearrange("p (t d) -> p t d", d=D),
                  x_h.ap()[t0 * 128:t1 * 128, :].rearrange(
                      "(t p) d -> p t d", p=128))
        for t in range(t0, t1):
            ld_x[t] = tok

    for k in range(12):
        s, kk = (0, k) if k < 8 else (1, k - 8)
        em.ins("tensor", lambda e, s=s, kk=kk, k=k: e.transpose(
            ps[5][:, k * nC:(k + 1) * nC],
            wldf[s][0:nC, kk * 128:(kk + 1) * 128], id_f[0:nC, 0:nC]),
            wait=[ld_sc] + ([("VE", ve_id)] if k == 0 else []),
            inc=("TE" if k == 11 else None))
    te_vec = em.c("TE")
    ve_incols = 0

    em.barrier()

    spill_st = []
    wldf_ve = [[("TE", te_vec)], [("TE", te_vec), ("VE", ve_id)],
               [("TE", te_vec)], [("TE", te_vec)]]
    wpar = [0]
    pkid = [0]

    def ffnw_pack(wi_ap, nblk, dst_ap, ve_q):
        myid = pkid[0]
        pkid[0] += 1
        bank = ps[7].bitcast(BF16)
        wt = wts[myid % 2]
        for r0 in range(0, nblk, 4):
            jw = min(4, nblk - r0)
            for j in range(jw):
                em.ins("tensor", lambda e, j=j, r0=r0, bank=bank, wi_ap=wi_ap:
                       e.transpose(
                           bank[:, j * 128:(j + 1) * 128],
                           wi_ap[:, (r0 + j) * 128:(r0 + j + 1) * 128], id_b[:, :]),
                       wait=([("VE", ve_q), ("SE", spill_evac_se[0])]
                             if j == 0 else []),
                       inc=("TE" if j == jw - 1 else None))
            te = em.c("TE")
            em.ins("scalar", lambda e, wt=wt, bank=bank, r0=r0, jw=jw: e.copy(
                wt[:, r0 * 128:(r0 + jw) * 128], bank[:, 0:jw * 128]),
                wait=[("TE", te)] +
                     ([spill_st[myid - 2]] if r0 == 0 and myid >= 2 else []),
                inc="SE")
            spill_evac_se[0] = em.c("SE")
        se = em.c("SE")
        stv = dma("gpsimd", f"WS{myid % 2}", dst_ap.rearrange("c p j -> p c j"),
                  wt[:, 0:nblk * 128].rearrange("p (c j) -> p c j", j=128),
                  wait=[("SE", se)])
        spill_st.append(stv)

    spill_evac_se = [0]

    gu_state = {}

    def gu_load(kind, it):
        wh = wg_h if kind == "g" else wu_h
        par = wpar[0] % 4
        wpar[0] += 1
        ld = dma("sync", f"WF{par}", wldf[par][:, 0:D],
                 wh.ap().rearrange("(i p) d -> i p d", p=128)[it],
                 wait=list(wldf_ve[par]))
        gu_state[(kind, it)] = {"par": par, "ld": ld}

    guq = [0]
    wintf_ve = [[], []]

    def gu_quant(kind, it):
        st = gu_state[(kind, it)]
        par = st["par"]
        wq = guq[0] % 2
        guq[0] += 1
        st["wq"] = wq
        rm = rmg if kind == "g" else rmu
        rmc = rm[:, it:it + 1]
        tr = tmpc[:, NSC - 2:NSC - 1]
        em.ins("vector", lambda e, rmc=rmc, par=par: e.reduce_max(
            rmc, wldf[par][:, 0:D], AX.X, apply_absolute_value=True),
            wait=[st["ld"]])
        em.ins("vector", lambda e, rmc=rmc, tr=tr: e.tensor_scalar_mul(tr, rmc, 1.0 / QMAX))
        em.ins("vector", lambda e, tr=tr: e.reciprocal(tr, tr))
        em.ins("vector", lambda e, par=par, tr=tr: e.tensor_scalar(
            wldf[par][:, 0:D], wldf[par][:, 0:D], tr, RND, ALU.mult, ALU.add))
        em.ins("vector", lambda e, par=par: e.tensor_scalar(
            wldf[par][:, 0:D], wldf[par][:, 0:D], RND, -QMAX - 1.0,
            ALU.subtract, ALU.max))
        em.ins("vector", lambda e, par=par, wq=wq: e.tensor_single_scalar(
            wintf[wq][:, 0:D], wldf[par][:, 0:D], QMAX, ALU.min),
            wait=list(wintf_ve[wq]), inc="VE")
        st["veq"] = em.c("VE")
        wldf_ve[par] = [("VE", st["veq"])]

    def gu_pack(kind, it):
        st = gu_state.pop((kind, it))
        sp = wT_sp[kind]
        ffnw_pack(wintf[st["wq"]][:, 0:nC * 128], nC, sp.ap()[it], st["veq"])
        wintf_ve[st["wq"]] = [("TE", em.c("TE"))]

    def ffnw_d(th):
        tr = tmpc[:, NSC - 2:NSC - 1]
        rsd = tmpc[:, NSC - 1:NSC]
        pars = []
        base = 0
        for ci, cl in enumerate(ICH):
            par = wpar[0] % 4
            wpar[0] += 1
            ld = dma("sync", f"WF{par}", wldf[par][:, 0:cl],
                     wd_h.ap()[th * 128:(th + 1) * 128, base:base + cl],
                     wait=list(wldf_ve[par]))
            dstc = rmd[:, th:th + 1] if ci == 0 else tmpc[:, NSC - 3:NSC - 2]
            em.ins("vector", lambda e, dstc=dstc, par=par, cl=cl: e.reduce_max(
                dstc, wldf[par][:, 0:cl], AX.X, apply_absolute_value=True),
                wait=[ld])
            if ci > 0:
                em.ins("vector", lambda e, th=th: e.tensor_max(
                    rmd[:, th:th + 1], rmd[:, th:th + 1], tmpc[:, NSC - 3:NSC - 2]))
            em.ins("vector", lambda e: e.engine_nop(), inc="VE")
            pars.append((par, cl, base))
            base += cl
        em.ins("vector", lambda e, th=th, tr=tr: e.tensor_scalar_mul(
            tr, rmd[:, th:th + 1], 1.0 / QMAX))
        em.ins("vector", lambda e, tr=tr: e.reciprocal(tr, tr))
        em.ins("vector", lambda e, th=th, rsd=rsd: e.tensor_scalar_mul(
            rsd, rmd[:, th:th + 1], 1.0 / QMAX), inc="VE")
        for ci2, (par, cl, base) in enumerate(pars):
            wq2 = ci2 % 2
            em.ins("vector", lambda e, par=par, cl=cl, tr=tr: e.tensor_scalar(
                wldf[par][:, 0:cl], wldf[par][:, 0:cl], tr, RND, ALU.mult, ALU.add))
            em.ins("vector", lambda e, par=par, cl=cl: e.tensor_scalar(
                wldf[par][:, 0:cl], wldf[par][:, 0:cl], RND, -QMAX - 1.0,
                ALU.subtract, ALU.max))
            em.ins("vector", lambda e, par=par, cl=cl: e.tensor_single_scalar(
                wldf[par][:, 0:cl], wldf[par][:, 0:cl], QMAX, ALU.min))
            em.ins("vector", lambda e, par=par, cl=cl, rsd=rsd, wq2=wq2:
                   e.tensor_scalar_mul(
                       wintf[wq2][:, 0:cl], wldf[par][:, 0:cl], rsd),
                   wait=list(wintf_ve[wq2]), inc="VE")
            ve_q = em.c("VE")
            nblk = cl // 128
            ic0 = base // 128
            ffnw_pack(wintf[wq2][:, 0:nblk * 128], nblk,
                      wT_sp["d"].ap()[ic0:ic0 + nblk, th], ve_q)
            wintf_ve[wq2] = [("TE", em.c("TE"))]
            wldf_ve[par] = [("VE", ve_q)]

    gu_units = [("g", it) for it in range(nI)] + [("u", it) for it in range(nI)]
    ffnw_sched = []
    for i in range(len(gu_units) + 2):
        if i < len(gu_units):
            ffnw_sched.append(("L",) + gu_units[i])
        if i >= 1 and i - 1 < len(gu_units):
            ffnw_sched.append(("Q",) + gu_units[i - 1])
        if i >= 2 and i - 2 < len(gu_units):
            ffnw_sched.append(("P",) + gu_units[i - 2])
    ffnw_sched += [("D", None, th) for th in range(nC)]
    ffnw_units = ffnw_sched
    ffnw_pos = [0]

    def ffnw_tick(n=4):
        flush = n > 8
        for _ in range(n):
            if ffnw_pos[0] >= len(ffnw_sched):
                return
            stg, kind, idx = ffnw_sched[ffnw_pos[0]]
            if stg == "D" and not flush:
                return
            ffnw_pos[0] += 1
            if stg == "L":
                gu_load(kind, idx)
            elif stg == "Q":
                gu_quant(kind, idx)
            elif stg == "P":
                gu_pack(kind, idx)
            else:
                ffnw_d(idx)

    wslot = {"v": 0, "o": 1, "k": 1, "q": 0}
    ve_w = {}
    wlda_ve = [0, 0, 0]
    slot_ve = {0: 0, 1: 0}

    def conv_attn_w(nm, extra=()):
        slot = wslot[nm]
        ws = w_h[nm].ap().rearrange("(c p) d -> c p d", p=128)
        for c in range(nC):
            sl = c % 3
            ld = dma("scalar", ("WA0", "WA2", "WA3")[sl], wlda[sl][:, 0:D], ws[c],
                     wait=[("VE", wlda_ve[sl])] + (list(extra) if c == 0 else []))
            em.ins("vector", lambda e, nm=nm, slot=slot, c=c, sl=sl: e.tensor_copy(
                wb[slot][:, c * D:(c + 1) * D], wlda[sl][:, 0:D]),
                wait=[ld, ("TE", slot_ve[slot])], inc="VE")
            wlda_ve[sl] = em.c("VE")
        ve_w[nm] = em.c("VE")

    wvo = [(nm, c) for nm in ("v", "o") for c in range(nC)]
    wvo_ld = {}
    for i in range(min(4, len(wvo))):
        nm, c = wvo[i]
        par = i % 4
        wvo_ld[i] = (par, dma("scalar", f"WF{par}", wldf[par][:, 0:D],
                              w_h[nm].ap()[c * 128:(c + 1) * 128, :],
                              wait=list(wldf_ve[par])))

    def emit_wvo_casts(i0, i1):
        for i in range(i0, i1):
            nm, c = wvo[i]
            slot = wslot[nm]
            par, ld = wvo_ld[i]
            em.ins("vector", lambda e, slot=slot, c=c, par=par: e.tensor_copy(
                wb[slot][:, c * D:(c + 1) * D], wldf[par][:, 0:D]),
                wait=[ld], inc="VE")
            wldf_ve[par] = [("VE", em.c("VE"))]
            ve_w[nm] = em.c("VE")
            j = i + 4
            if j < len(wvo):
                nm2, c2 = wvo[j]
                par2 = j % 4
                wvo_ld[j] = (par2, dma("scalar", f"WF{par2}", wldf[par2][:, 0:D],
                                       w_h[nm2].ap()[c2 * 128:(c2 + 1) * 128, :],
                                       wait=list(wldf_ve[par2])))


    sucol1 = stat[:, 0:nT]
    sscol1 = stat[:, nT:2 * nT]
    mcol1 = stat[:, 3 * nT:4 * nT]
    rcol1 = stat[:, 4 * nT:5 * nT]
    se_sq_t = [0] * nT
    ve_ln = [0] * nT
    LG = min(4, nT)

    def emit_afcols_unit(j):
        g = incols[:, (4 * j + 0) * nC:(4 * j + 1) * nC]
        b = incols[:, (4 * j + 1) * nC:(4 * j + 2) * nC]
        m = incols[:, (4 * j + 2) * nC:(4 * j + 3) * nC]
        v = incols[:, (4 * j + 3) * nC:(4 * j + 4) * nC]
        s4 = afcols[:, (2 * j) * nC:(2 * j + 1) * nC]
        b4 = afcols[:, (2 * j + 1) * nC:(2 * j + 2) * nC]
        veps = tmpc[:, 3 * nC:4 * nC]
        em.ins("vector", lambda e, v=v, veps=veps: e.tensor_scalar_add(
            veps, v, EPS), wait=[("VE", ve_incols)], inc="VE")
        rsqrt_cols(s4, veps, nC, 0)
        em.ins("vector", lambda e, s4=s4, g=g: e.tensor_mul(s4, s4, g))
        em.ins("vector", lambda e, b4=b4, m=m, s4=s4: e.tensor_mul(b4, m, s4))
        em.ins("vector", lambda e, b4=b4, b=b: e.tensor_sub(b4, b, b4))
        em.ins("vector", lambda e, b4=b4: e.tensor_scalar(
            b4, b4, 4.0, -4.0, ALU.mult, ALU.add))
        em.ins("vector", lambda e, s4=s4: e.tensor_scalar_mul(s4, s4, 4.0), inc="VE")

    pk = min(4, nC)
    se_x1 = 0
    ln1_done_te = 0
    bankrot = [5, 6]
    bi = 0
    prev_evac_se = {5: 0, 6: 0}
    first_b5 = [True]
    x1T3 = xln1T.ap().rearrange("p (c q) -> p c q", q=T)
    te_x_t = [0] * nT
    se_xt = {}
    TPT = TOKC // 128

    def emit_ln1t(t):
        nonlocal bi
        for c0 in range(0, nC, pk):
            cw = min(pk, nC - c0)
            bank = ps[bankrot[bi % 2]]
            bno = bankrot[bi % 2]
            bi += 1
            for j in range(cw):
                em.ins("tensor", lambda e, c0=c0, t=t, j=j, bank=bank: e.transpose(
                    bank[:, j * 128:(j + 1) * 128],
                    xbuf[:, t * D + (c0 + j) * 128:t * D + (c0 + j + 1) * 128],
                    id_f[:, :]),
                    wait=[("VE", ve_ln[t])] +
                         ([("SE", prev_evac_se[bno])] if j == 0 else []) +
                         ([("VE", ve_incols)] if first_b5[0] and j == 0 else []),
                    inc=("TE" if j == cw - 1 else None))
                first_b5[0] = False
            te = em.c("TE")
            em.ins("scalar", lambda e, c0=c0, cw=cw, t=t, bank=bank: e.copy(
                x1T3[:, c0:c0 + cw, t * 128:(t + 1) * 128],
                bank[:, 0:cw * 128].rearrange("p (c j) -> p c j", j=128)),
                wait=[("TE", te)], inc="SE")
            prev_evac_se[bno] = em.c("SE")
        te_x_t[t] = em.c("TE")
        if (t + 1) % TPT == 0:
            se_xt[(t + 1) // TPT - 1] = em.c("SE")

    nga = 0
    for g0 in range(0, nT, LG):
        for t in range(g0, g0 + LG):
            xt = xbuf[:, t * D:(t + 1) * D]
            em.ins("vector", lambda e, xt=xt, t=t: e.reduce_sum(
                sucol1[:, t:t + 1], xt, AX.X), wait=[ld_x[t]])
            em.ins("scalar", lambda e, xt=xt, t=t: e.activation(
                sqd[:, 0:D], xt, AF.Square, accum_out=sscol1[:, t:t + 1]),
                wait=[ld_x[t], ("VE", em.c("VE"))], inc="SE")
            se_sq_t[t] = em.c("SE")
        se_sq1 = se_sq_t[g0 + LG - 1]
        sg0 = slice(g0, g0 + LG)
        em.ins("vector", lambda e, sg0=sg0: e.tensor_scalar_mul(
            mcol1[:, sg0], sucol1[:, sg0], 1.0 / D), wait=[("SE", se_sq1)])
        em.ins("vector", lambda e, sg0=sg0: e.tensor_scalar_mul(
            rcol1[:, sg0], sscol1[:, sg0], 1.0 / D))
        em.ins("vector", lambda e, sg0=sg0: e.tensor_mul(
            sucol1[:, sg0], mcol1[:, sg0], mcol1[:, sg0]))
        em.ins("vector", lambda e, sg0=sg0: e.tensor_sub(
            rcol1[:, sg0], rcol1[:, sg0], sucol1[:, sg0]))
        em.ins("vector", lambda e, sg0=sg0: e.tensor_scalar_add(
            rcol1[:, sg0], rcol1[:, sg0], EPS), inc="VE")
        rsqrt_cols(rcol1[:, sg0], rcol1[:, sg0], LG, 0)
        for t in range(g0, g0 + LG):
            xt = xbuf[:, t * D:(t + 1) * D]
            em.ins("vector", lambda e, xt=xt, t=t: e.tensor_scalar(
                xt, xt, mcol1[:, t:t + 1], rcol1[:, t:t + 1],
                ALU.subtract, ALU.mult), inc="VE")
            ve_ln[t] = em.c("VE")
        if nga == 0:
            em.ins("vector", lambda e: e.tensor_copy(
                incols[:, :], ps[5][:, 0:12 * nC]),
                wait=[("TE", te_vec)], inc="VE")
            ve_incols = em.c("VE")
        if nga < 3:
            emit_afcols_unit(nga)
            nga += 1
        for t in range(g0, g0 + LG):
            emit_ln1t(t)
        if g0 == 0:
            emit_wvo_casts(0, nC)
    while nga < 3:
        emit_afcols_unit(nga)
        nga += 1
    ve_afcols = em.c("VE")
    se_x1 = em.c("SE")
    ln1_done_te = em.c("TE")

    emit_wvo_casts(nC, 2 * nC)

    qbuf_te_free = [0]
    row_extra_wait = []

    def emit_proj(nm, jaf, out_kind):
        wsl = wb[wslot[nm]]
        s4 = afcols[:, (2 * jaf) * nC:(2 * jaf + 1) * nC]
        b4 = afcols[:, (2 * jaf + 1) * nC:(2 * jaf + 2) * nC]
        dst = vtb if out_kind == "vt" else qbuf
        evac_se = [0] * nTC
        if out_kind == "vt":
            vto, _ = L.range_of("vtb")
            xo, xe = L.range_of("xbuf")
        for c in range(nC):
            for tc in range(nTC):
                bank = ps[tc]
                w0 = []
                if c == 0:
                    if out_kind == "vt":
                        w0 = [("SE", se_xt[tc])]
                        if tc == 0:
                            w0.append(_wv(ve_w[nm]))
                    elif tc == 0:
                        w0 = [_wv(ve_w[nm]), ("SE", se_x1)]
                if c > 0:
                    w0.append(("SE", evac_se[tc]))
                for ci in range(nC):
                    em.ins("tensor", lambda e, wsl=wsl, c=c, ci=ci, tc=tc, bank=bank: e.matmul(
                        bank[:, 0:TOKC],
                        wsl[:, ci * D + c * 128:ci * D + (c + 1) * 128],
                        xln1T[:, ci * T + tc * TOKC:ci * T + (tc + 1) * TOKC],
                        start=(ci == 0), stop=(ci == nC - 1)),
                        wait=(w0 if ci == 0 else []),
                        inc=("TE" if ci == nC - 1 else None))
                te = em.c("TE")
                wE = [("TE", te)]
                if c == 0 and tc == 0:
                    wE.append(("VE", ve_afcols))
                if out_kind == "vt":
                    wb0 = vto + (c * T + tc * TOKC) * 2
                    wb1 = wb0 + TOKC * 2
                    if wb0 < xe and wb1 > xo:
                        t_hi = min(nT - 1, max(0, (min(wb1, xe) - 1 - xo) // (D * 4)))
                        wE.append(("TE", te_x_t[t_hi]))
                if out_kind == "row" and c == 0 and tc == 0:
                    wE.append(("TE", qbuf_te_free[0]))
                    wE.extend(row_extra_wait)
                em.ins("scalar", lambda e, c=c, tc=tc, bank=bank, s4=s4, b4=b4, dst=dst:
                       e.activation(
                           dst[:, c * T + tc * TOKC:c * T + (tc + 1) * TOKC],
                           bank[:, 0:TOKC], AF.Sigmoid,
                           bias=b4[:, c:c + 1], scale=s4[:, c:c + 1]),
                       wait=wE, inc="SE")
                evac_se[tc] = em.c("SE")
            ffnw_tick()
        if out_kind == "row":
            row = rowpool[0:1, 0:T] if nm == "q" else rowpool[32:33, 0:T]
            ve_row = 0
            for tc in range(nTC):
                for c in range(nC):
                    em.ins("tensor", lambda e, c=c, tc=tc: e.matmul(
                        ps[4][0:1, 0:TOKC], ones_cb[:, :],
                        qbuf[:, c * T + tc * TOKC:c * T + (tc + 1) * TOKC],
                        start=(c == 0), stop=(c == nC - 1)),
                        wait=(([("SE", evac_se[tc])] if c == 0 else []) +
                              ([("VE", ve_row)] if c == 0 and tc > 0 else [])),
                        inc=("TE" if c == nC - 1 else None))
                tte = em.c("TE")
                em.ins("vector", lambda e, tc=tc, row=row: e.tensor_scalar(
                    row[:, tc * TOKC:(tc + 1) * TOKC], ps[4][0:1, 0:TOKC],
                    -1.0 / D, 1.0, ALU.mult, ALU.add),
                    wait=[("TE", tte)], inc="VE")
                ve_row = em.c("VE")
            qbuf_te_free[0] = em.c("TE")

    use_kstage = (T >= 2 * D)
    if use_kstage:
        kstage = qbuf.bitcast(F32)
        ld_kst = 0
        for c in range(nC):
            ld_kst = dma("scalar", "WA1", kstage[:, c * D:(c + 1) * D],
                         w_h["k"].ap()[c * 128:(c + 1) * 128, :],
                         wait=([("SE", se_x1)] if c == 0 else []))

    emit_proj("v", 2, "vt")
    se_vt_all = em.c("SE")

    slot0_free = em.c("TE")
    wsq = w_h["q"].ap().rearrange("(c p) d -> c p d", p=128)
    for c in range(nC):
        sl = c % 3
        ld = dma("scalar", ("WA0", "WA2", "WA3")[sl], wlda[sl][:, 0:D], wsq[c],
                 wait=[("VE", wlda_ve[sl])])
        em.ins("vector", lambda e, c=c, sl=sl: e.tensor_copy(
            wb[0][:, c * D:(c + 1) * D], wlda[sl][:, 0:D]),
            wait=[ld, ("TE", slot0_free)] +
                 ([("SE", se_x1)] if c == 0 else []), inc="VE")
        wlda_ve[sl] = em.c("VE")
    ve_w["q"] = em.c("VE")

    st_vw = [0] * nT
    ve_vw = [0] * nT
    for t in range(nT):
        par = t % 2
        for h in range(nOC):
            bank = ps[(t % 2) * 2 + h]
            w0 = []
            if t == 0 and h == 0:
                w0 = [("SE", se_vt_all), ("VE", ve_w["o"])]
            if t >= 2:
                w0.append(ve_vw[t - 2])
            for cd in range(nC):
                em.ins("tensor", lambda e, t=t, h=h, cd=cd, bank=bank: e.matmul(
                    bank[:, 0:OC],
                    vtb[:, cd * T + t * 128:cd * T + (t + 1) * 128],
                    wb[wslot["o"]][:, cd * D + h * OC:cd * D + (h + 1) * OC],
                    start=(cd == 0), stop=(cd == nC - 1)),
                    wait=(w0 if cd == 0 else []),
                    inc=("TE" if cd == nC - 1 else None))
            te = em.c("TE")
            em.ins("scalar", lambda e, par=par, h=h, bank=bank: e.copy(
                vwring[par][:, h * OC:(h + 1) * OC], bank[:, 0:OC]),
                wait=[("TE", te)] + ([st_vw[t - 2]] if t >= 2 and h == 0 else []),
                inc="SE")
        ve_vw[t] = ("SE", em.c("SE"))
        st_vw[t] = dma("scalar", f"VS{par}", vw_in.ap()[t * 128:(t + 1) * 128, :],
                       vwring[par][:, 0:D], wait=[ve_vw[t]])
        for h in range(nOC):
            em.ins("tensor", lambda e, par=par, h=h, t=t: e.matmul(
                ps[5 + h][0:1, 0:OC], ones_cb[:, :],
                vwring[par][:, h * OC:(h + 1) * OC],
                start=(t == 0), stop=(t == nT - 1)),
                wait=([ve_vw[t]] if h == 0 else []),
                inc=("TE" if h == nOC - 1 else None))
        ffnw_tick()
    te_cs_all = em.c("TE")
    em.ins("vector", lambda e: e.tensor_copy(sqd[0:1, 0:OC], ps[5][0:1, 0:OC]),
           wait=[("TE", te_cs_all)])
    if nOC > 1:
        em.ins("vector", lambda e: e.tensor_copy(sqd[0:1, OC:D], ps[6][0:1, 0:OC]))
    em.ins("vector", lambda e: e.engine_nop(), inc="VE")
    ve_cse = em.c("VE")
    st_cs = dma("scalar", "CSS", cs_in.ap().rearrange("(a b) -> a b", a=1),
                sqd[0:1, 0:D], wait=[("VE", ve_cse)])
    em.ins("gpsimd", lambda e: e.collective_compute(
        "AllGather", ALU.bypass, replica_groups=rg_batch,
        ins=[vw_in.ap().opt()], outs=[vw_out.ap().opt()]),
        wait=[st_vw[nT - 2], st_vw[nT - 1]], inc="CC1")
    cc_vw = ("CC1", em.c("CC1"))
    em.ins("gpsimd", lambda e: e.collective_compute(
        "AllReduce", ALU.add, replica_groups=rg_batch,
        ins=[cs_in.ap().opt()], outs=[cs_out.ap().opt()]),
        wait=[st_cs], inc="CC4")
    cc_cs = ("CC4", em.c("CC4"))

    slot_ve[0] = em.c("TE")
    slot_ve[1] = em.c("TE")
    row_extra_wait.extend([("VS0", em.c("VS0")), ("VS1", em.c("VS1"))])
    if use_kstage:
        for c in range(nC):
            em.ins("scalar", lambda e, c=c: e.copy(
                wb[1][:, c * D:(c + 1) * D], kstage[:, c * D:(c + 1) * D]),
                wait=[ld_kst, ("TE", slot_ve[1])], inc="SE")
        ve_w["k"] = ("SE", em.c("SE"))
    else:
        conv_attn_w("k")
    emit_proj("k", 1, "row")
    ve_tk = em.c("VE")
    st_tk = dma("scalar", "TKS", tk_in.ap().rearrange("(a b) -> a b", a=1),
                rowpool[32:33, 0:T], wait=[("VE", ve_tk)])
    em.ins("gpsimd", lambda e: e.collective_compute(
        "AllGather", ALU.bypass, replica_groups=rg_batch,
        ins=[tk_in.ap().opt()], outs=[tk_out.ap().opt()]),
        wait=[st_tk], inc="CC0")
    cc_tk = ("CC0", em.c("CC0"))
    emit_proj("q", 0, "row")
    ve_tq = em.c("VE")
    ffnw_tick(len(ffnw_units))
    st_spill_all = [("WS0", em.c("WS0")), ("WS1", em.c("WS1"))]
    ld_tk = dma("scalar", "TKL", tkld[0:nK, 0:128],
                tk_out.ap().rearrange("(a b) -> a b", b=128),
                wait=[cc_tk])

    em.barrier()

    em.ins("tensor", lambda e: e.transpose(
        ps[5][0:128, 0:nK], tkld[0:nK, 0:128], id_f[0:nK, 0:nK]),
        wait=[ld_tk], inc="TE")
    te_tk = em.c("TE")
    em.ins("vector", lambda e: e.tensor_copy(tK_sb[:, :], ps[5][0:128, 0:nK]),
           wait=[("TE", te_tk)])
    em.ins("vector", lambda e: e.tensor_scalar_mul(tK2p[:, :], tK_sb[:, :], 2.0))
    em.ins("vector", lambda e: e.tensor_scalar_mul(tK2n[:, :], tK_sb[:, :], -2.0),
           inc="VE")
    ve_tkt = em.c("VE")
    ve_tqc = {}
    for tc in range(nTC):
        bank = ps[4 + (tc % 2)]
        w = [("VE", ve_tkt)] if tc == 1 else []
        if tc >= 2:
            w.append(("VE", ve_tqc[tc - 2]))
        em.ins("tensor", lambda e, tc=tc, bank=bank: e.matmul(
            bank[:, 0:TOKC], ones_rf[:, :], rowpool[0:1, tc * TOKC:(tc + 1) * TOKC],
            start=True, stop=True), wait=w, inc="TE")
        te = em.c("TE")
        em.ins("vector", lambda e, tc=tc, bank=bank: e.tensor_copy(
            tQb[:, tc * TOKC:(tc + 1) * TOKC], bank[:, 0:TOKC]),
            wait=[("TE", te)], inc="VE")
        ve_tqc[tc] = em.c("VE")
    ve_tqb = em.c("VE")
    ld_bo = dma("sync", "VEC", borow[:, :], bo_h.ap().rearrange("(a b) -> a b", a=1),
                wait=st_spill_all)
    ld_cs = dma("sync", "CSL", crow[:, :],
                cs_out.ap().rearrange("(a b) -> a b", a=1),
                wait=[cc_cs] + st_spill_all)
    em.ins("vector", lambda e: e.scalar_tensor_tensor(
        crow[:, 0:D], borow[:, 0:D], 1.0 / 0.9, crow[:, 0:D],
        ALU.mult, ALU.add), wait=[ld_cs, ld_bo], inc="VE")
    ve_crow = em.c("VE")

    vws = vw_out.ap().rearrange("(kt p) d -> kt p d", p=128)
    kstep = max(1, nK // 8)
    ld_vw_c = {}
    for ci, k0 in enumerate(range(0, nK, kstep)):
        k1 = min(nK, k0 + kstep)
        v = dma("sync", f"VWL{ci}",
                vw_sb[:, k0 * D:k1 * D].rearrange("p (k d) -> p k d", d=D),
                vw_out.ap()[k0 * 128:k1 * 128, :].rearrange(
                    "(k p) d -> p k d", p=128),
                wait=(([cc_vw] + st_spill_all) if k0 == 0 else []))
        for k in range(k0, k1):
            ld_vw_c[k] = v
    ld_vw = v

    sucol = stat[:, 0:nT]
    sscol = stat[:, nT:2 * nT]
    mcol = stat[:, 3 * nT:4 * nT]
    rcol = stat[:, 4 * nT:5 * nT]
    amcol = stat[:, 5 * nT:6 * nT]
    r2i = stat[:, 6 * nT:7 * nT]
    amr = stat[:, 7 * nT:7 * nT + 1]
    NB = nT - QG if nT > QG else 0
    te_av = {}
    ve_d = {}
    ge_d = {}
    se_d = {}
    ve_evac = [0] * nT
    ve_banks = [0] * nT
    st_xmid = [0] * nT
    pend_stats = []
    batchA_q = []
    def emit_dbuild(j):
        qg_, kt_ = divmod(j, nK)
        par = j % 4
        tkp = tK2p[:, kt_:kt_ + 1]
        tkn = tK2n[:, kt_:kt_ + 1]
        tkc = tK_sb[:, kt_:kt_ + 1]
        tqs = tQb[:, qg_ * QG * 128:(qg_ + 1) * QG * 128]
        wS = [("VE", ve_d[j - 4])] if j >= 4 else [("VE", ve_tqb), ("VE", ve_tkt)]
        em.ins("scalar", lambda e, par=par, tqs=tqs, tkp=tkp: e.activation(
            ub[par][:, :], tqs, AF.Exp, bias=tkp, scale=-2.0), wait=wS)
        em.ins("scalar", lambda e, par=par, tqs=tqs, tkn=tkn: e.activation(
            wub[par][:, :], tqs, AF.Exp, bias=tkn, scale=2.0), inc="SE")
        se_d[j] = em.c("SE")
        em.ins("vector", lambda e, par=par: e.tensor_scalar_mul(
            db[par][:, :], ub[par][:, :], -1.0),
            wait=[("SE", se_d[j])] + ([("TE", te_av[j - 4])] if j >= 4 else []))
        em.ins("vector", lambda e, par=par, tqs=tqs, tkc=tkc: e.tensor_single_scalar(
            mb[par][:, :], tqs, tkc, ALU.is_lt))
        em.ins("vector", lambda e, par=par: e.copy_predicated(
            db[par][:, :], mb[par][:, :], wub[par][:, :]), inc="VE")
        ve_d[j] = em.c("VE")

    for qg in range(nQG):
        for qj in range(QG):
            for h in range(nOC):
                bank = ps[qj * nOC + h]
                w1 = [("VE", ve_crow)] if qg == 0 and qj == 0 and h == 0 else []
                if qg > 0:
                    w1.append(("VE", ve_banks[(qg - 1) * QG + qj]))
                em.ins("tensor", lambda e, qj=qj, h=h, bank=bank: e.matmul(
                    bank[:, 0:OC], ones_rf[:, :], crow[:, h * OC:(h + 1) * OC],
                    start=True, stop=False), wait=w1,
                    inc=("TE" if (qj == QG - 1 and h == nOC - 1) else None))
        for kt in range(nK):
            j = qg * nK + kt
            par = j % 4
            if j not in ve_d:
                emit_dbuild(j)
            if kt == 4 and pend_stats:
                pend_stats.pop(0)()
                if qg == nQG - 1 and NB > 0:
                    batchA_q.extend([
                        lambda: em.ins("vector", lambda e: e.tensor_scalar(
                            rcol[:, 0:NB], sscol[:, 0:NB], 1.0 / D, EPS,
                            ALU.mult, ALU.add),
                            wait=[("VE", ve_evac[NB - 1])]),
                        lambda: em.ins("vector", lambda e: e.reciprocal(
                            r2i[:, 0:NB], rcol[:, 0:NB])),
                        lambda: em.ins("vector", lambda e: e.tensor_mul(
                            amcol[:, 0:NB], amcol[:, 0:NB], r2i[:, 0:NB]),
                            inc="VE"),
                        lambda: rsqrt_cols(rcol[:, 0:NB], rcol[:, 0:NB], NB, 0),
                    ])
            elif batchA_q:
                batchA_q.pop(0)()
            w0 = [("VE", ve_d[j])]
            if qg == 0 and kt % kstep == 0:
                w0.append(ld_vw_c[kt])
            for qj in range(QG):
                for h in range(nOC):
                    bank = ps[qj * nOC + h]
                    last = (qj == QG - 1 and h == nOC - 1)
                    em.ins("tensor", lambda e, par=par, qj=qj, h=h, kt=kt, bank=bank:
                           e.matmul(
                               bank[:, 0:OC],
                               db[par][:, qj * 128:(qj + 1) * 128],
                               vw_sb[:, kt * D + h * OC:kt * D + (h + 1) * OC],
                               start=False, stop=(kt == nK - 1)),
                           wait=(w0 if (qj == 0 and h == 0) else []),
                           inc=("TE" if last else None))
            te_av[j] = em.c("TE")
        te_qg = em.c("TE")
        for qj in range(QG):
            t = qg * QG + qj
            par = t % 4
            ldx = dma("sync", f"XR{t % 2}", xrel[t % 2][:, 0:D], xs[t],
                      wait=[("VE", ve_evac[t - 2])] if t >= 2 else [])
            for h in range(nOC):
                bank = ps[qj * nOC + h]
                acc = tmpc[:, 8 + 2 * qj + h:9 + 2 * qj + h]
                em.ins("vector", lambda e, par=par, t=t, h=h, bank=bank, acc=acc:
                       e.scalar_tensor_tensor(
                           xmr[par][:, h * OC:(h + 1) * OC], bank[:, 0:OC], 0.9,
                           xrel[t % 2][:, h * OC:(h + 1) * OC], ALU.mult, ALU.add,
                           accum_out=acc),
                       wait=[("TE", te_qg), ldx] +
                            ([st_xmid[t - 4], ("VE", ve_evac[t - 4])]
                             if t >= 4 else []))
            em.ins("vector", lambda e: e.engine_nop(), inc="VE")
            ve_banks[t] = em.c("VE")
            st_xmid[t] = dma("scalar", "XM2" if t >= nT - QG else f"XM{t % 2}",
                             xmid_h.ap()[t * 128:(t + 1) * 128, :],
                             xmr[par][:, 0:D], wait=[("VE", ve_banks[t])])
            if qg + 1 < nQG and qj < min(4, nK):
                emit_dbuild((qg + 1) * nK + qj)

        def mk_stats(qg):
            def emit():
                for qj in range(QG):
                    t = qg * QG + qj
                    par = t % 4
                    su = stat[:, 0 * nT + t:0 * nT + t + 1]
                    ssq = stat[:, 1 * nT + t:1 * nT + t + 1]
                    mcol = stat[:, 3 * nT + t:3 * nT + t + 1]
                    am = stat[:, 5 * nT + t:5 * nT + t + 1]
                    a0 = tmpc[:, 8 + 2 * qj:9 + 2 * qj]
                    a1 = tmpc[:, 9 + 2 * qj:10 + 2 * qj] if nOC > 1 else None
                    if nOC > 1:
                        em.ins("vector", lambda e, su=su, a0=a0, a1=a1: e.tensor_add(
                            su, a0, a1), wait=[("VE", ve_banks[t])])
                    else:
                        em.ins("vector", lambda e, su=su, a0=a0: e.tensor_copy(
                            su, a0), wait=[("VE", ve_banks[t])])
                    em.ins("vector", lambda e, su=su, mcol=mcol: e.tensor_scalar_mul(
                        mcol, su, -1.0 / D), inc="VE")
                    ve_m = em.c("VE")
                    em.ins("scalar", lambda e, par=par, ssq=ssq, mcol=mcol:
                           e.activation(
                               sqd[:, 0:D], xmr[par][:, 0:D], AF.Square,
                               bias=mcol, accum_out=ssq),
                        wait=[("VE", ve_m)], inc="SE")
                    se_sq = em.c("SE")
                    em.ins("vector", lambda e, am=am: e.reduce_max(
                        am, sqd[:, 0:D], AX.X), wait=[("SE", se_sq)], inc="VE")
                    ve_evac[t] = em.c("VE")
            return emit
        pend_stats.append(mk_stats(qg))

    te_avl = em.c("TE")
    while batchA_q:
        batchA_q.pop(0)()

    def armax_pre(idx, ve_ready):
        em.ins("tensor", lambda e: e.transpose(ps[6][0:1, 0:128], amr, id_f[:, :]),
               wait=[("VE", ve_ready)], inc="TE")
        te = em.c("TE")
        em.ins("vector", lambda e: e.reduce_max(p0ex[:, 2:3], ps[6][0:1, 0:128], AX.X),
               wait=[("TE", te)])
        for i in range(8):
            em.ins("vector", lambda e, i=i: e.tensor_copy(
                p0ex[:, 8 + i:9 + i], p0ex[:, 2:3]))
        em.ins("vector", lambda e: e.engine_nop(), inc="VE")
        ve = em.c("VE")
        st = dma("scalar", f"MXS{idx}", mx_in[idx].ap().rearrange("(a b) -> a b", a=1),
                 p0ex[:, 8:16], wait=[("VE", ve)])
        ccs = "CC2" if idx == 0 else "CC3"
        em.ins("gpsimd", lambda e, idx=idx: e.collective_compute(
            "AllReduce", ALU.max, replica_groups=rg_all,
            ins=[mx_in[idx].ap().opt()], outs=[mx_out[idx].ap().opt()]),
            wait=[st], inc=ccs)
        return (ccs, em.c(ccs))

    def armax_post(idx, clip_cell, cc, sq=False):
        ld = dma("sync", f"MXL{idx}", p0ex[:, 16:24],
                 mx_out[idx].ap().rearrange("(a b) -> a b", a=1), wait=[cc])
        em.ins("scalar", lambda e, clip_cell=clip_cell: e.activation(
            p0ex[:, 3:4], p0ex[:, clip_cell:clip_cell + 1], AF.Sigmoid),
            wait=[ld, ld_sc], inc="SE")
        gsrc = 16
        if sq:
            em.ins("scalar", lambda e: e.activation(
                p0ex[:, 6:7], p0ex[:, 16:17], AF.Sqrt), inc="SE")
            gsrc = 6
        se = em.c("SE")
        em.ins("vector", lambda e, gsrc=gsrc: e.tensor_mul(
            p0ex[:, 4:5], p0ex[:, 3:4], p0ex[:, gsrc:gsrc + 1]), wait=[("SE", se)])
        em.ins("vector", lambda e: e.tensor_scalar_mul(p0ex[:, 4:5], p0ex[:, 4:5],
                                                       1.0 / QMAX))
        em.ins("vector", lambda e: e.reciprocal(p0ex[:, 5:6], p0ex[:, 4:5]), inc="VE")
        ve2 = em.c("VE")
        for z, (src_, dc) in enumerate(((5, 2 * idx), (4, 2 * idx + 1))):
            em.ins("tensor", lambda e, src_=src_, dc=dc: e.matmul(
                ps[6][:, 128 + dc:129 + dc], ones_rf[:, :], p0ex[:, src_:src_ + 1],
                start=True, stop=True),
                wait=([("VE", ve2)] if z == 0 else []),
                inc=("TE" if z == 1 else None))
        te2 = em.c("TE")
        em.ins("vector", lambda e, idx=idx: e.tensor_copy(
            ccol[:, 2 * idx:2 * idx + 2], ps[6][:, 128 + 2 * idx:130 + 2 * idx]),
            wait=[("TE", te2)], inc="VE")
        return em.c("VE")

    xms = xmid_h.ap().rearrange("(t p) d -> t p d", p=128)
    rxT3 = rxT.ap().rearrange("p (c t) -> p c t", t=T) if nC > 1 else None
    ve_ap2 = [0] * nT
    rx_te = [0] * nT
    rx_bank_se = [0, 0]

    def _ovl(a, b):
        (a0, a1), (b0, b1) = L.range_of(a), L.range_of(b)
        return a0 < b1 and b0 < a1

    rxb_xmr = any(_ovl(f"rxb{i}", f"xmr{j}")
                  for i in range(2) for j in range(4))

    def emit_xln2t(t, pre=False):
        par = t % 2
        xm_w = ([st_xmid[nT - QG - 2], st_xmid[nT - QG - 1]]
                if (NB > 0 and t < NB) else [st_xmid[nT - 1]])
        ldx = dma("sync", f"X2{par}", xrel2[par][:, 0:D], xms[t],
                  wait=xm_w +
                       ([("TE", te_avl), ("VE", ve_crow)] if pre else []) +
                       ([("VE", ve_ap2[t - 2])] if t >= 2 else []))
        mc = stat[:, 3 * nT + t:3 * nT + t + 1]
        rc = stat[:, 4 * nT + t:4 * nT + t + 1]
        rb = rxb[par]
        em.ins("vector", lambda e, par=par, mc=mc, rc=rc, rb=rb: e.tensor_scalar(
            rb[:, 0:D], xrel2[par][:, 0:D], mc, rc, ALU.add, ALU.mult),
            wait=[ldx] +
                 ([("TE", rx_te[t - 2])] if t >= 2
                  else [("TE", te_avl)] +
                       ([st_xmid[nT - 1]] if rxb_xmr else [])) +
                 (st_spill_all if t == 0 else []), inc="VE")
        ve_ap2[t] = em.c("VE")
        for c0 in range(0, nC, 4):
            cw = min(4, nC - c0)
            bno = (c0 // 4) % 2
            bank = ps[4 + bno].bitcast(BF16)
            wj = [("VE", ve_ap2[t]), ("SE", rx_bank_se[bno])]
            if pre and t < 2 and 4 + bno < QG * nOC:
                wj.append(("VE", ve_banks[(nQG - 1) * QG + (4 + bno) // nOC]))
            for j in range(cw):
                em.ins("tensor", lambda e, rb=rb, c0=c0, j=j, bank=bank: e.transpose(
                    bank[:, j * 128:(j + 1) * 128],
                    rb[:, (c0 + j) * 128:(c0 + j + 1) * 128], id_b[:, :]),
                    wait=(wj if j == 0 else []),
                    inc=("TE" if j == cw - 1 else None))
            te = em.c("TE")
            if nC > 1:
                dst = rxT3[:, c0:c0 + cw, t * 128:(t + 1) * 128]
                srcb = bank[:, 0:cw * 128].rearrange("p (c j) -> p c j", j=128)
            else:
                dst = rxT[:, t * 128:(t + 1) * 128]
                srcb = bank[:, 0:128]
            em.ins("scalar", lambda e, dst=dst, srcb=srcb: e.copy(dst, srcb),
                   wait=[("TE", te)], inc="SE")
            rx_bank_se[bno] = em.c("SE")
        rx_te[t] = em.c("TE")

    xl2_early = NB >= 2 and not rxb_xmr
    if xl2_early:
        emit_xln2t(0, pre=True)
        emit_xln2t(1, pre=True)
    while pend_stats:
        pend_stats.pop(0)()
    em.ins("vector", lambda e: e.tensor_scalar(
        rcol[:, NB:nT], sscol[:, NB:nT], 1.0 / D, EPS, ALU.mult, ALU.add),
        wait=[("VE", ve_evac[nT - 1])])
    em.ins("vector", lambda e: e.reciprocal(r2i[:, NB:nT], rcol[:, NB:nT]))
    em.ins("vector", lambda e: e.tensor_mul(
        amcol[:, NB:nT], amcol[:, NB:nT], r2i[:, NB:nT]))
    em.ins("vector", lambda e: e.reduce_max(amr, amcol, AX.X), inc="VE")
    ve_am = em.c("VE")
    cc_ar1 = armax_pre(0, ve_am)
    rsqrt_cols(rcol[:, NB:nT], rcol[:, NB:nT], nT - NB, 0)
    if not xl2_early:
        emit_xln2t(0, pre=True)
        if nT > 1:
            emit_xln2t(1, pre=True)
    for t in range(2, nT):
        emit_xln2t(t, pre=True)
    se_rx_all = em.c("SE")

    em.barrier()

    ve_s1 = armax_post(0, 0, cc_ar1, sq=True)
    em.ins("vector", lambda e: e.tensor_scalar_mul(cgc[:, :], rmg[:, :], ccol[:, 1:2]))
    em.ins("vector", lambda e: e.tensor_scalar_mul(cgc[:, :], cgc[:, :], 1.0 / QMAX))
    em.ins("vector", lambda e: e.tensor_scalar_mul(cuc[:, :], rmu[:, :], ccol[:, 1:2]))
    em.ins("vector", lambda e: e.tensor_scalar_mul(cuc[:, :], cuc[:, :], 1.0 / QMAX),
           inc="VE")
    ve_c1 = em.c("VE")

    HQ2 = min(1024, T)
    qsl = [sqd[:, 0:HQ2], xrel2[0][:, 0:HQ2]]
    ve_rx_hc = [0] * nC
    ve_qtok = []
    ki = 0
    for hc in range(nC):
        for q0 in range(0, T, HQ2):
            sl = rxT[:, hc * T + q0:hc * T + q0 + HQ2]
            qs = qsl[ki % 2]
            em.ins("scalar", lambda e, sl=sl, qs=qs: e.activation(
                qs, sl, AF.Copy, bias=RND, scale=ccol[:, 0:1]),
                wait=([("VE", ve_s1)] if ki == 0 else []) +
                     ([("VE", ve_qtok[ki - 2])] if ki >= 2 else []), inc="SE")
            se_q = em.c("SE")
            em.ins("vector", lambda e, qs=qs: e.tensor_scalar(
                qs, qs, RND, -QMAX - 1.0, ALU.subtract, ALU.max),
                wait=[("SE", se_q)])
            em.ins("vector", lambda e, sl=sl, qs=qs: e.tensor_single_scalar(
                sl, qs, QMAX, ALU.min), inc="VE")
            ve_qtok.append(em.c("VE"))
            ki += 1
        ve_rx_hc[hc] = em.c("VE")
    ve_rx = em.c("VE")

    ld_wff = [0] * nI
    ve_h = [0] * nI
    se_h = [0] * nI
    te_gu = [0] * nI
    hamax = stat[:, 2 * nT:2 * nT + 1]
    hsc = stat[:, 2 * nT + 1:2 * nT + 2]
    se_silu = [[0] * nTC for _ in range(nI)]
    ve_hstt = [[0] * nTC for _ in range(nI)]
    for it in range(nI):
        par = it % 2
        w_g = wff[par][:, 0:nC * 128]
        w_u = wff[par][:, nC * 128:2 * nC * 128]
        wld = list(st_spill_all)
        if it >= 2:
            wld += [("TE", te_gu[it - 2])]
        dma("sync", f"WG{par}", w_g.rearrange("p (c j) -> p c j", j=128),
            wT_sp["g"].ap()[it].rearrange("c p j -> p c j"), wait=wld)
        ld_wff[it] = dma("sync", f"WG{par}", w_u.rearrange("p (c j) -> p c j", j=128),
                         wT_sp["u"].ap()[it].rearrange("c p j -> p c j"))
        for tc in range(nTC):
            for z, wsl in ((0, w_g), (1, w_u)):
                bank = ps[z * 4 + tc]
                for hc in range(nC):
                    w0 = []
                    if it == 0:
                        w0.append(("VE", ve_rx_hc[hc]))
                    if hc == 0:
                        if z == 0 and tc == 0:
                            w0 = w0 + [ld_wff[it]]
                        if it >= 1:
                            w0.append(("SE", se_silu[it - 1][tc]) if z == 0
                                      else ("VE", ve_hstt[it - 1][tc]))
                    em.ins("tensor", lambda e, wsl=wsl, hc=hc, tc=tc, bank=bank:
                           e.matmul(
                               bank[:, 0:TOKC], wsl[:, hc * 128:(hc + 1) * 128],
                               rxT[:, hc * T + tc * TOKC:hc * T + (tc + 1) * TOKC],
                               start=(hc == 0), stop=(hc == nC - 1)),
                           wait=w0,
                           inc=("TE" if hc == nC - 1 else None))
                if z == 0:
                    te_g = em.c("TE")
                else:
                    te_u = em.c("TE")
            em.ins("scalar", lambda e, it=it, tc=tc: e.activation(
                sg[:, tc * TOKC:(tc + 1) * TOKC], ps[tc][:, 0:TOKC], AF.Silu,
                scale=cgc[:, it:it + 1]),
                wait=[("TE", te_g)] +
                     ([("VE", ve_hstt[it - 1][tc])] if it >= 1 else []), inc="SE")
            se_silu[it][tc] = em.c("SE")
            em.ins("vector", lambda e, it=it, tc=tc: e.scalar_tensor_tensor(
                h_sb[:, it * T + tc * TOKC:it * T + (tc + 1) * TOKC],
                ps[4 + tc][:, 0:TOKC], cuc[:, it:it + 1],
                sg[:, tc * TOKC:(tc + 1) * TOKC], ALU.mult, ALU.mult),
                wait=[("SE", se_silu[it][tc]), ("TE", te_u)])
            em.ins("vector", lambda e, it=it, tc=tc: e.reduce_max(
                hsc, h_sb[:, it * T + tc * TOKC:it * T + (tc + 1) * TOKC],
                AX.X, apply_absolute_value=True))
            if it == 0 and tc == 0:
                em.ins("vector", lambda e: e.tensor_copy(hamax, hsc), inc="VE")
            else:
                em.ins("vector", lambda e: e.tensor_max(hamax, hamax, hsc), inc="VE")
            ve_hstt[it][tc] = em.c("VE")
        te_gu[it] = em.c("TE")
        ve_h[it] = em.c("VE")
        se_h[it] = em.c("SE")
    ve_h_all = em.c("VE")
    se_h_all = em.c("SE")
    te_gu_all = em.c("TE")

    em.ins("vector", lambda e: e.tensor_copy(amr, hamax), wait=[("VE", ve_h_all)],
           inc="VE")
    cc_ar2 = armax_pre(1, em.c("VE"))

    ld_wdt = [0] * nI
    for ic in range(nI):
        ld_wdt[ic] = dma("sync", "WDL",
                         wdT_sb[:, ic * D:(ic + 1) * D].rearrange(
                             "p (c j) -> p c j", j=128),
                         wT_sp["d"].ap()[ic].rearrange("c p j -> p c j"),
                         wait=(st_spill_all + [("SE", se_h_all),
                               ("VE", ve_h_all), ("TE", te_gu_all)]
                               if ic == 0 else []))

    ve_s2 = armax_post(1, 1, cc_ar2)

    ve_rh = [0] * nI
    HQ = min(1024, T)
    for it in range(nI):
        for q0 in range(0, T, HQ):
            em.ins("scalar", lambda e, it=it, q0=q0: e.activation(
                vd2[:, 0:HQ], h_sb[:, it * T + q0:it * T + q0 + HQ], AF.Copy,
                bias=RND, scale=ccol[:, 2:3]),
                wait=[("VE", ve_s2)] + ([("VE", ve_rh[it - 1])] if q0 == 0 and it > 0
                                        else []),
                inc="SE")
            se = em.c("SE")
            em.ins("vector", lambda e: e.tensor_scalar(
                vd2[:, 0:HQ], vd2[:, 0:HQ], RND, -QMAX - 1.0, ALU.subtract, ALU.max),
                wait=[("SE", se)])
            em.ins("vector", lambda e, it=it, q0=q0: e.tensor_single_scalar(
                h_sb[:, it * T + q0:it * T + q0 + HQ], vd2[:, 0:HQ], QMAX, ALU.min),
                inc="VE")
        ve_rh[it] = em.c("VE")

    ve_y = [0] * nT
    st_y = [0] * nT
    for tt in range(nT):
        par = tt % 2
        ldx = dma("sync", f"X2{par}", xrel2[par][:, 0:D], xms[tt],
                  wait=[("VE", ve_rx)] +
                       ([("VE", ve_y[tt - 2])] if tt >= 2 else []))
        for h in range(nOC):
            bank = ps[(tt % 3) * 2 + h]
            for ic in range(nI):
                w0 = []
                if tt == 0:
                    w0 = [("VE", ve_rh[ic]), ld_wdt[nI - 1]]
                    if ic == 0 and h == 0:
                        w0.append(("SE", se_h_all))
                elif ic == 0 and h == 0 and tt >= 3:
                    w0 = [("VE", ve_y[tt - 3])]
                em.ins("tensor", lambda e, tt=tt, h=h, ic=ic, bank=bank: e.matmul(
                    bank[:, 0:OC],
                    h_sb[:, ic * T + tt * 128:ic * T + (tt + 1) * 128],
                    wdT_sb[:, ic * D + h * OC:ic * D + (h + 1) * OC],
                    start=(ic == 0), stop=(ic == nI - 1)),
                    wait=w0, inc=("TE" if ic == nI - 1 else None))
            te = em.c("TE")
            em.ins("vector", lambda e, par=par, h=h, bank=bank: e.scalar_tensor_tensor(
                ybuf[par][:, h * OC:(h + 1) * OC], bank[:, 0:OC], ccol[:, 3:4],
                xrel2[par][:, h * OC:(h + 1) * OC], ALU.mult, ALU.add),
                wait=[("TE", te), ldx] +
                     ([st_y[tt - 2]] if tt >= 2 and h == 0 else []),
                inc=("VE" if h == nOC - 1 else None))
        ve_y[tt] = em.c("VE")
        st_y[tt] = dma("scalar", f"YS{par}", out_h.ap()[tt * 128:(tt + 1) * 128, :],
                       ybuf[par][:, 0:D], wait=[("VE", ve_y[tt])])

    em.wait("scalar", "YS0", em.c("YS0"))
    em.wait("scalar", "YS1", em.c("YS1"))
    em.barrier()

    with ExitStack() as stk:
        em.sems = {s: stk.enter_context(nc.semaphore(s)) for s in SEMS}
        with nc.Block() as block:
            em.replay(block)

    return nc


def kernel(**inputs):
    cfg = full_cfg()
    T, GROUP, CORES = cfg["T"], cfg["GROUP"], cfg["CORES"]
    nc = build_program(cfg)
    x = np.ascontiguousarray(np.asarray(inputs["L0_attn_x"], dtype=np.float32))
    Bsz, N, D = x.shape
    shared = {k: np.ascontiguousarray(np.asarray(v, dtype=np.float32))
              for k, v in inputs.items() if k != "L0_attn_x"}
    in_maps = []
    for core in range(CORES):
        b, q = core // GROUP, core % GROUP
        m = dict(shared)
        m["L0_attn_x"] = np.ascontiguousarray(x[b, q * T:(q + 1) * T, :])
        in_maps.append(m)
    res = run_bass_kernel_spmd(nc, in_maps, core_ids=list(range(CORES)))
    out = np.empty((Bsz, N, D), dtype=np.float32)
    for core in range(CORES):
        b, q = core // GROUP, core % GROUP
        out[b, q * T:(q + 1) * T, :] = res.results[core]["out"]
    return out
